# version 09
# baseline (speedup 1.0000x reference)
import contextlib
import math
import numpy as np
import ml_dtypes
import concourse.bass as bass
import concourse.mybir as mybir
from concourse.bass_utils import run_bass_kernel_spmd

F32 = mybir.dt.float32
BF16 = mybir.dt.bfloat16
AF = mybir.ActivationFunctionType
ALU = mybir.AluOpType

PI = float(np.pi)
MAGIC = 12582912.0
EPS = 1e-5
DECAY = 0.03
RFF_SCALE = math.sqrt(2.0 / 64.0) * math.sqrt(math.exp(-0.5))
MAGS = [math.sqrt(abs((-1.0) ** m / math.factorial(m)) + 1e-10)
        for m in range(5)]


def _trig_coeffs():
    xs = np.cos((np.arange(4000) + 0.5) * np.pi / 4000) * np.pi
    A = np.stack([xs ** k for k in range(14)], 1)
    cs, *_ = np.linalg.lstsq(A, np.sin(xs), rcond=None)
    return [float(cs[k]) for k in (1, 3, 5, 7, 9, 11)]


SIN_C = _trig_coeffs()


def make_cfg(seq_len):
    nqb = seq_len // 128
    return dict(SEQ=seq_len, NQB=nqb, NKB=nqb + 4, NV=12, QLAG=10, VLAG=2,
                NITER=nqb + 12)


def host_constants(cfg):
    jw = (np.arange(5)[:, None, None] * 128 + np.arange(128)[None, :, None])
    i = np.arange(128)[None, None, :]
    dist = 512 + i - jw
    twt = np.where(dist >= 0, np.exp(-DECAY * np.maximum(dist, 0.0)), 0.0)
    TWT = np.ascontiguousarray(
        twt.transpose(1, 0, 2).reshape(128, 640)).astype(np.float32)
    maskT = (np.arange(128)[:, None] <= np.arange(128)[None, :]
             ).astype(ml_dtypes.bfloat16)
    E = np.zeros((128, 256), ml_dtypes.bfloat16)
    E[:, 155] = 1
    return dict(
        TWT=TWT, maskT=maskT, E=E,
        ident=np.eye(128, dtype=ml_dtypes.bfloat16),
        identf=np.eye(128, dtype=np.float32),
        allones=np.ones((128, 128), ml_dtypes.bfloat16),
        ones_row=np.ones((1, 128), ml_dtypes.bfloat16),
        ones_rowf=np.ones((1, 128), np.float32),
        ones_colf=np.ones((128, 1), np.float32),
        ones_col=np.ones((128, 1), ml_dtypes.bfloat16),
        wpri=np.arange(8, 0, -1, dtype=np.float32)[None, :],
    )


def stage_weights(w):
    out = {}
    out["wv"] = np.ascontiguousarray(w["L0_attn_W_v"].T)
    out["wout"] = np.ascontiguousarray(w["L0_attn_W_out"].T)
    out["wsc"] = np.ascontiguousarray(np.concatenate(
        [w["L0_attn_W_shell"].T, w["L0_attn_W_charge"].T], axis=1))
    out["bsc_row"] = np.ascontiguousarray(np.concatenate(
        [w["L0_attn_b_shell"], w["L0_attn_b_charge"]])[None, :])
    out["omegaT"] = np.ascontiguousarray(w["L0_attn_omega"].T)
    wc = w["L0_attn_W_combine"]
    wcT = np.zeros((128, 3, 64), np.float32)
    for c in range(3):
        n = min(128, 320 - 128 * c)
        wcT[:n, c, :] = wc[:, 128 * c:128 * c + n].T
    out["wcT"] = np.ascontiguousarray(wcT.reshape(128, 192))
    out["rff4"] = np.ascontiguousarray(np.repeat(
        np.tile(np.asarray(w["L0_attn_rff_bias"], np.float32), 4)[None, :],
        128, axis=0))
    out["b_v_T"] = np.ascontiguousarray(
        np.asarray(w["L0_attn_b_v"], np.float32).reshape(8, 128).T)
    out["b_out_row"] = np.ascontiguousarray(w["L0_attn_b_out"][None, :])
    out["log_temp"] = np.full((1, 1), float(w["L0_attn_log_temp"]), np.float32)
    out["gT"] = np.ascontiguousarray(
        np.asarray(w["L0_ffn_ln_g"], np.float32).reshape(8, 128).T)
    out["lnbT"] = np.ascontiguousarray(
        np.asarray(w["L0_ffn_ln_b"], np.float32).reshape(8, 128).T)
    out["ln_g_row"] = np.ascontiguousarray(w["L0_ffn_ln_g"][None, :])
    out["ln_b_row"] = np.ascontiguousarray(w["L0_ffn_ln_b"][None, :])
    out["bn_mean_row"] = np.ascontiguousarray(w["L0_ffn_bn_mean"][None, :])
    out["bn_var_row"] = np.ascontiguousarray(w["L0_ffn_bn_var"][None, :])
    out["bn_g_row"] = np.ascontiguousarray(w["L0_ffn_bn_g"][None, :])
    out["bn_b_row"] = np.ascontiguousarray(w["L0_ffn_bn_b"][None, :])
    out["wr"] = np.ascontiguousarray(
        np.asarray(w["L0_ffn_Wr"], np.float32).reshape(8, 128, 8)
        .transpose(1, 0, 2).reshape(128, 64))
    out["br_row"] = np.ascontiguousarray(w["L0_ffn_br"][None, :])
    out["wdown"] = np.ascontiguousarray(
        np.asarray(w["L0_ffn_W_down"], np.float32)
        .reshape(8, 8, 128, 64).transpose(2, 1, 0, 3).reshape(128, 4096))
    out["bdown8"] = np.ascontiguousarray(w["L0_ffn_b_down"])
    out["wup"] = np.ascontiguousarray(
        np.asarray(w["L0_ffn_W_up"], np.float32)
        .transpose(1, 0, 2).reshape(64, 8192))
    out["bup8"] = np.ascontiguousarray(w["L0_ffn_b_up"])
    return out


IN_SPECS = {
    "wv": ([1024, 1024], F32), "wout": ([1024, 1024], F32),
    "wsc": ([1024, 65], F32), "bsc_row": ([1, 65], F32),
    "omegaT": ([64, 32], F32), "wcT": ([128, 192], F32),
    "rff4": ([128, 128], F32), "b_v_T": ([128, 8], F32),
    "b_out_row": ([1, 1024], F32), "log_temp": ([1, 1], F32),
    "gT": ([128, 8], F32), "lnbT": ([128, 8], F32),
    "ln_g_row": ([1, 1024], F32), "ln_b_row": ([1, 1024], F32),
    "bn_mean_row": ([1, 1024], F32), "bn_var_row": ([1, 1024], F32),
    "bn_g_row": ([1, 1024], F32), "bn_b_row": ([1, 1024], F32),
    "wr": ([128, 64], F32), "br_row": ([1, 8], F32),
    "wdown": ([128, 4096], F32), "bdown8": ([8, 64], F32),
    "wup": ([64, 8192], F32), "bup8": ([8, 1024], F32),
    "TWT": ([128, 640], F32), "wpri": ([1, 8], F32),
    "ones_rowf": ([1, 128], F32), "ones_colf": ([128, 1], F32),
    "identf": ([128, 128], F32),
    "maskT": ([128, 128], BF16), "E": ([128, 256], BF16),
    "ident": ([128, 128], BF16), "allones": ([128, 128], BF16),
    "ones_row": ([1, 128], BF16), "ones_col": ([128, 1], BF16),
}


class _Rec:

    def __init__(self, eng):
        object.__setattr__(self, "_e", eng)
        object.__setattr__(self, "last", None)

    def __getattr__(self, n):
        f = getattr(object.__getattribute__(self, "_e"), n)
        if callable(f):
            def wrap(*a, **k):
                r = f(*a, **k)
                if r is not None and hasattr(r, "then_inc"):
                    object.__setattr__(self, "last", r)
                return r
            return wrap
        return f

    def inc(self, sem, n=1):
        object.__getattribute__(self, "last").then_inc(sem, n)


def build_nc(cfg):
    NQB, NKB, NV, QLAG = cfg["NQB"], cfg["NKB"], cfg["NV"], cfg["QLAG"]
    NITER, SEQ = cfg["NITER"], cfg["SEQ"]
    VLAG = cfg["VLAG"]
    NGRP = (NITER - VLAG + 3) // 4

    nc = bass.Bass("TRN2", target_bir_lowering=False,
                   detect_race_conditions=False)
    x_k = nc.dram_tensor("x_k", [NKB * 128, 1024], F32,
                         kind="ExternalInput").ap()
    y_out = nc.dram_tensor("y_out", [NQB * 128, 1024], F32,
                           kind="ExternalOutput").ap()
    zhat_dram = nc.dram_tensor("zhat_scr", [NQB * 128, 1024], BF16,
                               kind="Internal").ap()
    rb_dram = nc.dram_tensor("rb_dram", [1, 1024], F32, kind="Internal").ap()
    gin = {n: nc.dram_tensor(n, s, d, kind="ExternalInput").ap()
           for n, (s, d) in IN_SPECS.items()}
    dbg_outs = {}

    es = contextlib.ExitStack()
    sbt = lambda n, s, d: es.enter_context(nc.sbuf_tensor(n, s, d))
    pstt = lambda n, s: es.enter_context(nc.psum_tensor(n, s, F32))
    semt = lambda n: es.enter_context(nc.semaphore(n))

    with es:
        xbuf = sbt("xbuf", [128, 3, 1024], F32)
        xqbuf = sbt("xqbuf", [128, 2, 1024], F32)
        xinb = sbt("xinb", [128, 2, 1024], BF16)
        xinT = sbt("xinT", [128, 4, 8, 128], BF16)
        vwin = sbt("vwin", [128, NV, 1024], BF16)
        phiT = sbt("phiT", [64, NKB * 128], BF16)
        S_sb = sbt("S_sb", [72, 1024], BF16)
        zsum4 = sbt("zsum4", [4, 1024], F32)
        oa_bf = sbt("oa_bf", [128, 2, 1024], BF16)
        oaT = sbt("oaT", [128, 2, 8, 128], BF16)
        PTt = sbt("PTt", [128, 640], BF16)
        scf = sbt("scf", [128, 640], F32)
        t2buf = sbt("t2buf", [128, 1024], F32)
        y1buf = sbt("y1buf", [128, 1024], F32)
        zhat_sb = sbt("zhat_sb", [128, 2, 1024], BF16)
        stats1 = sbt("stats1", [128, NQB, 2], F32)
        y2buf = sbt("y2buf", [128, 2, 1024], F32)
        stage = y2buf[:, 0, :]
        stage2 = sbt("stage2", [128, 512], F32)
        wv_b = sbt("wv_b", [128, 8, 1024], BF16)
        wout_b = sbt("wout_b", [128, 8, 1024], BF16)
        wsc_b = sbt("wsc_b", [128, 8, 65], BF16)
        bscr_b = sbt("bscr_b", [1, 65], BF16)
        omegaT_b = sbt("omegaT_b", [64, 32], BF16)
        wcT_b = sbt("wcT_b", [128, 3, 64], BF16)
        wr_b = sbt("wr_b", [128, 8, 8], BF16)
        wdown_b = sbt("wdown_b", [128, 8, 8, 64], BF16)
        wup_b = sbt("wup_b", [64, 8, 1024], BF16)
        bup8_b = sbt("bup8_b", [8, 1024], BF16)
        bdown8_b = sbt("bdown8_b", [8, 64], BF16)
        lnbT_b = sbt("lnbT_b", [128, 8], BF16)
        gTc = sbt("gTc", [128, 8], F32)
        bvc_b = sbt("bvc_b", [128, 8], BF16)
        weff = sbt("weff", [128, 2, 8, 64], BF16)
        wdg_b = sbt("wdg_b", [128, 8, 64], BF16)
        wupe = sbt("wupe", [64, 2, 1024], BF16)
        bup_eb = sbt("bup_eb", [1, 1024], BF16)
        bdcol = sbt("bdcol", [64, 1], F32)
        cvec_b = sbt("cvec_b", [1, 1024], BF16)
        cTWT = sbt("cTWT", [128, 640], F32)
        cmaskT = sbt("cmaskT", [128, 128], BF16)
        cE = sbt("cE", [128, 256], BF16)
        cidentf = sbt("cidentf", [128, 128], F32)
        callones = sbt("callones", [128, 128], BF16)
        cones_row = sbt("cones_row", [1, 128], BF16)
        cones_col = sbt("cones_col", [128, 1], BF16)
        cones_rowf = sbt("cones_rowf", [1, 128], F32)
        cones_colf = sbt("cones_colf", [128, 1], F32)
        cwpri = sbt("cwpri", [1, 8], F32)
        crtemp = sbt("crtemp", [128, 1], F32)
        crff = sbt("crff", [128, 128], F32)
        ltmp = sbt("ltmp", [1, 4], F32)
        mvk = sbt("mvk", [128, 2, 16], F32)
        muk = sbt("muk", [128, 2, 1], F32)
        rvk = sbt("rvk", [128, 2, 1], F32)
        rstdk = sbt("rstdk", [128, 2, 1], F32)
        nmrk = sbt("nmrk", [128, 2, 1], F32)
        n2cb = sbt("n2cb", [128, 2, 1], F32)
        e2q = sbt("e2q", [128, 2, 1], F32)
        qvv = sbt("qvv", [128, 2, 1], F32)
        pcs = sbt("pcs", [128, 2, 1], F32)
        rnorm = sbt("rnorm", [128, 2, 1], F32)
        pc4 = sbt("pc4", [128, 4, 5], F32)
        shellU = sbt("shellU", [128, 2, 64], F32)
        shellUT = sbt("shellUT", [64, 2, 128], BF16)
        proj4s = sbt("proj4s", [128, 4, 32], F32)
        rrX = sbt("rrX", [128, 128], F32)
        rrY = sbt("rrY", [128, 128], F32)
        rrH = sbt("rrH", [128, 128], F32)
        rrT = sbt("rrT", [128, 128], F32)
        phis4 = sbt("phis4", [128, 4, 64], F32)
        tpt = sbt("tpt", [128, 2, 320], F32)
        tpTs = sbt("tpTs", [128, 3, 128], BF16)
        phtk = sbt("phtk", [128, 64], F32)
        zqb = sbt("zqb", [128, 2, 1], F32)
        rZb = sbt("rZb", [128, 2, 1], F32)
        mva = sbt("mva", [128, 2, 16], F32)
        mua = sbt("mua", [128, 2, 1], F32)
        rva = sbt("rva", [128, 2, 1], F32)
        rstda = sbt("rstda", [128, 2, 1], F32)
        nmra = sbt("nmra", [128, 2, 1], F32)
        mv1 = sbt("mv1", [128, 2, 16], F32)
        mu1 = sbt("mu1", [128, 2, 1], F32)
        rv1 = sbt("rv1", [128, 2, 1], F32)
        r1cb = sbt("r1cb", [128, 2, 1], F32)
        sg1 = sbt("sg1", [128, 2, 1], F32)
        nmr1 = sbt("nmr1", [128, 2, 1], F32)
        zhT2 = sbt("zhT2", [128, 2, 8, 128], BF16)
        hTt = sbt("hTt", [64, 2, 128], BF16)
        tb2 = sbt("tb2", [128, 1024], F32)
        sqs = y2buf[:, 1, 0:64]
        mvy = sbt("mvy", [128, 2, 16], F32)
        muy = sbt("muy", [128, 2, 1], F32)
        rvy = sbt("rvy", [128, 2, 1], F32)
        rYc = sbt("rYc", [128, 2, 1], F32)
        nmry = sbt("nmry", [128, 2, 1], F32)
        zbnT = sbt("zbnT", [128, 8], BF16)
        ohb = sbt("ohb", [128, 8], F32)
        gcol = sbt("gcol", [128, 1], F32)
        ohc = sbt("ohc", [8, 1], BF16)
        r8a = sbt("r8a", [1, 8], F32)
        r8b = sbt("r8b", [1, 8], F32)
        r8c = sbt("r8c", [1, 8], F32)
        r1a = sbt("r1a", [1, 1], F32)
        r1b = sbt("r1b", [1, 1], F32)

        ps_x = pstt("ps_x", [128, 1024])
        ps_y = pstt("ps_y", [128, 1024])
        ps_s = pstt("ps_s", [100, 1024])
        ps_a = pstt("ps_a", [128, 512])
        ps_e = pstt("ps_e", [128, 512])

        dbg_list = []
        if cfg.get("DEBUG"):
            cand = [
                ("crtemp", crtemp[:, :]), ("cvec", cvec_b[:, :]),
                ("phiT", phiT[:, 0:1024]), ("S_sb", S_sb[:, :]),
                ("vwinall", vwin[:, :, :].rearrange("p a b -> p (a b)")), ("PTt", PTt[:, :]),
                ("scf", scf[:, :]), ("stats1", stats1[:, :, :]
                 .rearrange("p a b -> p (a b)")),
                ("zhat", zhat_sb[:, 0, :]), ("t2", t2buf[:, :]),
                ("oa", oa_bf[:, 0, :]), ("zsum4", zsum4[:, :]),
                ("ohb", ohb[:, :]), ("gcol", gcol[:, :]),
                ("wdg", wdg_b[:, :, :].rearrange("p a b -> p (a b)")),
                ("bup_eb", bup_eb[:, :]), ("bdcol", bdcol[:, :]),
                ("wvb0", wv_b[:, 0, :]), ("xinT0", xinT[:, 0, :, :]
                 .rearrange("p a b -> p (a b)")),
                ("rZb", rZb[:, :, :].rearrange("p a b -> p (a b)")),
                ("y2b", y2buf[:, 0, :]), ("tb2", tb2[:, :]),
                ("hT", hTt[:, :, :].rearrange("p a b -> p (a b)")),
                ("zbnT", zbnT[:, :]),
                ("n2cb", n2cb[:, :, :].rearrange("p a b -> p (a b)")),
                ("rnorm", rnorm[:, :, :].rearrange("p a b -> p (a b)")),
                ("shellU", shellU[:, 0, :]),
                ("shellUT", shellUT[:, 0, :]),
                ("proj4s", proj4s[:, :, :].rearrange("p a b -> p (a b)")),
                ("phis4", phis4[:, :, :].rearrange("p a b -> p (a b)")),
                ("tpt", tpt[:, 0, :]),
                ("e2q", e2q[:, :, :].rearrange("p a b -> p (a b)")),
                ("qvv", qvv[:, :, :].rearrange("p a b -> p (a b)")),
                ("muk", muk[:, :, :].rearrange("p a b -> p (a b)")),
                ("rvk", rvk[:, :, :].rearrange("p a b -> p (a b)")),
                ("sqs", sqs[:, :]),
                ("tpTs", tpTs[:, :, :].rearrange("p a b -> p (a b)")),
                ("xinb", xinb[:, 0, :]),
                ("pc4", pc4[:, :, :].rearrange("p a b -> p (a b)")),
            ]
            for nm, ap in cand:
                sh = list(ap.shape)
                dbg_outs[nm] = nc.dram_tensor(
                    "dbg_" + nm, sh, ap.dtype, kind="ExternalOutput").ap()
                dbg_list.append((nm, ap))

        sem_names = [
            "s_setd", "s_setc", "s_x", "s_xq", "s_xt", "s_oat", "s_zsc",
            "s_zld", "s_zt", "s_out", "s_ln", "s_vmm", "s_vev", "s_sqr",
            "s_st0", "s_n2c", "s_phi", "s_sc", "s_pst", "s_exp", "s_msk",
            "s_pv", "s_zf", "s_pvev", "s_wmm", "s_ast", "s_wev", "s_y1",
            "s_st1", "s_zh", "s_zhs", "s_smm", "s_sev", "s_shx", "s_shT",
            "s_shev", "s_prj", "s_pjv", "s_tp", "s_tpe", "s_tpev", "s_phm",
            "s_ptk", "s_ptT", "s_r", "s_wl", "s_rb", "s_h", "s_hev", "s_ymm",
            "s_tg", "s_yst", "s_t3", "s_y2", "s_dve", "s_acd", "s_t2e",
        ]
        S = {n: semt(n) for n in sem_names}

        def kv(it):
            return it if it < NKB else None

        def k2v(it):
            k2 = it - VLAG
            return k2 if 0 <= k2 < NKB else None

        def bvf(it):
            b = it - QLAG
            return b if 0 <= b < NQB else None

        def bwf(it):
            b = it - QLAG - 1
            return b if 0 <= b < NQB else None

        def bzf(it):
            b = it - QLAG - 2
            return b if 0 <= b < NQB else None

        def cnt_b(it):
            return max(0, min(it - QLAG, NQB))

        def g2f(it):
            return (it - VLAG) // 4

        def grp_close(it):
            t = it - VLAG
            return t >= 0 and (t % 4 == 3 or it == NITER - 1)

        SETUP = []

        def sdma(engine, fn):
            SETUP.append((engine, "dma", fn))

        def scomp(engine, fn):
            SETUP.append((engine, "comp", fn))

        for nm, dst in [
            ("rff4", crff[:, :]), ("log_temp", ltmp[0:1, 0:1]),
            ("TWT", cTWT[:, :]), ("wpri", cwpri[:, :]),
            ("ones_rowf", cones_rowf[:, :]), ("ones_colf", cones_colf[:, :]),
            ("identf", cidentf[:, :]),
            ("maskT", cmaskT[:, :]), ("E", cE[:, :]),
            ("allones", callones[:, :]),
            ("ones_row", cones_row[:, :]), ("ones_col", cones_col[:, :]),
            ("gT", gTc[:, :]),
        ]:
            sdma("sync", lambda e, nm=nm, dst=dst:
                 e.dma_start(out=dst, in_=gin[nm][:, :]))
        smalls = [
            ("bsc_row", (0, 1, 0, 65), bscr_b[:, :]),
            ("omegaT", (0, 64, 96, 128), omegaT_b[:, :]),
            ("wcT", (0, 128, 128, 320),
             wcT_b[:, :, :].rearrange("p a b -> p (a b)")),
            ("wr", (0, 128, 320, 384),
             wr_b[:, :, :].rearrange("p a b -> p (a b)")),
            ("lnbT", (0, 128, 384, 392), lnbT_b[:, :]),
            ("bdown8", (0, 8, 392, 456), bdown8_b[:, :]),
            ("b_v_T", (0, 128, 456, 464), bvc_b[:, :]),
        ]
        for nm, (p0, p1, c0, c1), dst in smalls:
            sdma("sync", lambda e, nm=nm, p0=p0, p1=p1, c0=c0, c1=c1:
                 e.dma_start(out=stage2[p0:p1, c0:c1], in_=gin[nm][:, :]))
        for nm, (p0, p1, c0, c1), dst in smalls:
            scomp("scalar", lambda e, p0=p0, p1=p1, c0=c0, c1=c1, dst=dst:
                  e.activation(dst, stage2[p0:p1, c0:c1], AF.Copy))
        sdma("sync", lambda e: e.dma_start(out=stage[0:8, :],
                                           in_=gin["bup8"][:, :]))
        scomp("scalar", lambda e: e.activation(bup8_b[:, :], stage[0:8, :],
                                               AF.Copy))
        big = []
        for i in range(8):
            big.append((gin["wv"][128 * i:128 * (i + 1), :], stage[:, :],
                        wv_b[:, i, :]))
        for i in range(8):
            big.append((gin["wout"][128 * i:128 * (i + 1), :], stage[:, :],
                        wout_b[:, i, :]))
        big.append((gin["wsc"][:, :].rearrange("(c p) o -> p c o", p=128),
                    stage[:, 0:520].rearrange("p (c o) -> p c o", o=65),
                    wsc_b[:, :, :].rearrange("p a b -> p (a b)")))
        for i in range(4):
            big.append((gin["wdown"][:, 1024 * i:1024 * (i + 1)], stage[:, :],
                        wdown_b[:, 2 * i:2 * i + 2, :, :]
                        .rearrange("p a e o -> p (a e o)")))
        for i in range(8):
            big.append((gin["wup"][:, 1024 * i:1024 * (i + 1)],
                        stage[0:64, :], wup_b[0:64, i, :]))
        for src, st_ap, dst in big:
            sdma("sync", lambda e, src=src, st_ap=st_ap:
                 e.dma_start(out=st_ap, in_=src))
            flat = st_ap if len(st_ap.shape) == 2 else st_ap.rearrange(
                "p a b -> p (a b)")
            scomp("scalar", lambda e, st_ap=flat, dst=dst:
                  e.activation(dst, st_ap, AF.Copy))
        sdma("sync", lambda e: e.dma_start(out=stage[0:1, 0:1024],
                                           in_=gin["b_out_row"][:, :]))
        scomp("scalar", lambda e: e.activation(ltmp[0:1, 1:2],
                                               ltmp[0:1, 0:1], AF.Exp))
        scomp("vector", lambda e: e.tensor_scalar(ltmp[0:1, 1:2],
                                                  ltmp[0:1, 1:2], 1.0, None,
                                                  ALU.add))
        scomp("scalar", lambda e: e.activation(ltmp[0:1, 1:2],
                                               ltmp[0:1, 1:2], AF.Ln))
        scomp("vector", lambda e: e.tensor_scalar(ltmp[0:1, 1:2],
                                                  ltmp[0:1, 1:2], 0.1, None,
                                                  ALU.add))
        scomp("vector", lambda e: e.reciprocal(ltmp[0:1, 2:3],
                                               ltmp[0:1, 1:2]))
        scomp("tensor", lambda e: e.matmul(ps_e[:, 0:1], cones_rowf[:, :],
                                           ltmp[0:1, 2:3], start=True,
                                           stop=True))
        scomp("vector", lambda e: e.tensor_copy(crtemp[:, :], ps_e[:, 0:1]))

        def _cvec_mms(e):
            for dc in range(8):
                e.matmul(ps_y[0:1, 0:512], bvc_b[:, dc:dc + 1],
                         wout_b[:, dc, 0:512], start=(dc == 0),
                         stop=(dc == 7))
                e.matmul(ps_y[0:1, 512:1024], bvc_b[:, dc:dc + 1],
                         wout_b[:, dc, 512:1024], start=(dc == 0),
                         stop=(dc == 7))
        scomp("tensor", _cvec_mms)
        scomp("vector", lambda e: e.tensor_add(stage[0:1, 0:1024],
                                               ps_y[0:1, 0:1024],
                                               stage[0:1, 0:1024]))
        scomp("vector", lambda e: e.tensor_copy(cvec_b[:, :],
                                                stage[0:1, 0:1024]))
        scomp("vector", lambda e: e.memset(S_sb[:, :], 0.0))
        scomp("vector", lambda e: e.memset(zsum4[:, :], 0.0))
        scomp("vector", lambda e: e.memset(phiT[:, :], 0.0))

        SET_D = 16 * sum(1 for _, kk, _f in SETUP if kk == "dma")
        SET_C = sum(1 for _, kk, _f in SETUP if kk == "comp")

        def emit_setup(eng, engine_name):
            d = c = 0
            for engn, kind, fn in SETUP:
                if engn == engine_name:
                    if d:
                        eng.wait_ge(S["s_setd"], 16 * d)
                    if c:
                        eng.wait_ge(S["s_setc"], c)
                    r = fn(eng)
                    if r is None:
                        r = eng.last if eng.last is not None else eng.nop()
                    if kind == "dma":
                        r.then_inc(S["s_setd"], 16)
                    else:
                        r.then_inc(S["s_setc"], 1)
                if kind == "dma":
                    d += 1
                else:
                    c += 1

        def setup_barrier(eng):
            eng.wait_ge(S["s_setd"], SET_D)
            eng.wait_ge(S["s_setc"], SET_C)

        grp_last = {}
        for it in range(VLAG, NITER):
            g = g2f(it)
            if bzf(it) is not None:
                grp_last[g] = ("z", it)
            if k2v(it) is not None:
                grp_last[g] = ("s", it)
        smm_cnt = {}
        zhs_cnt = {}
        cs = cz = 0
        for it in range(NITER):
            if k2v(it) is not None:
                cs += 1
            if bzf(it) is not None:
                cz += 1
            smm_cnt[it] = cs
            zhs_cnt[it] = cz

        rctr = [0]
        wlc = [7]

        with nc.Block() as block:

            @block.sync
            def _(eng):
                eng = _Rec(eng)
                emit_setup(eng, "sync")
                setup_barrier(eng)
                for it in range(NITER):
                    k, b = kv(it), bvf(it)
                    if k is not None:
                        if k >= 3:
                            eng.wait_ge(S["s_ln"], k - 2)
                        eng.wait_ge(S["s_x"], 16 * k)
                        eng.dma_start(out=xbuf[:, k % 3, :],
                                      in_=x_k[128 * k:128 * (k + 1), :]
                                      ).then_inc(S["s_x"], 16)
                    if b is not None:
                        if b >= 2:
                            eng.wait_ge(S["s_y1"], b - 1)
                        eng.wait_ge(S["s_xq"], 16 * b)
                        eng.dma_start(
                            out=xqbuf[:, b % 2, :],
                            in_=x_k[512 + 128 * b:512 + 128 * (b + 1), :]
                        ).then_inc(S["s_xq"], 16)
                eng.wait_ge(S["s_sev"], NGRP)
                for wi, (nm, dst) in enumerate(
                        [("ln_g_row", xbuf[0:1, 0, 0:1024]),
                         ("ln_b_row", xbuf[0:1, 1, 0:1024]),
                         ("bn_mean_row", xbuf[0:1, 2, 0:1024]),
                         ("bn_var_row", y1buf[0:1, 0:1024]),
                         ("bn_g_row", xqbuf[0:1, 0, 0:1024]),
                         ("bn_b_row", xqbuf[0:1, 1, 0:1024]),
                         ("br_row", stage2[0:1, 0:8])]):
                    eng.wait_ge(S["s_wl"], 16 * wi)
                    eng.dma_start(out=dst, in_=gin[nm][:, :]
                                  ).then_inc(S["s_wl"], 16)
                eng.wait_ge(S["s_zsc"], 16 * NQB)
                for j in range(NQB):
                    if j >= 2:
                        eng.wait_ge(S["s_y2"], j - 1)
                    eng.wait_ge(S["s_zld"], 16 * j)
                    eng.dma_start(out=zhat_sb[:, j % 2, :],
                                  in_=zhat_dram[128 * j:128 * (j + 1), :]
                                  ).then_inc(S["s_zld"], 16)
                    eng.wait_ge(S["s_zld"], 16 * (j + 1))
                    eng.wait_ge(S["s_zt"], 16 * j)
                    if j >= 2:
                        eng.wait_ge(S["s_h"], j - 1)
                    eng.dma_start(out=zhT2[:, j % 2, :, :],
                                  in_=zhat_sb[:, j % 2, :]
                                  .rearrange("p (c f) -> p c f", f=128),
                                  transpose=True).then_inc(S["s_zt"], 16)

            @block.tensor
            def _(eng):
                eng = _Rec(eng)
                emit_setup(eng, "tensor")
                setup_barrier(eng)
                started = {}
                for it in range(NITER):
                    k, b, bz = k2v(it), bvf(it), bzf(it)
                    g = g2f(it)
                    if b is not None:
                        eng.wait_ge(S["s_msk"], b + 1)
                        eng.wait_ge(S["s_vev"], b + 5)
                        eng.wait_ge(S["s_pst"], b + 1)
                        eng.wait_ge(S["s_shx"], max(0, min(it - VLAG, NKB)))
                        eng.wait_ge(S["s_tpev"], 4 * max(g, 0))
                        if b >= 1:
                            eng.wait_ge(S["s_zf"], b)
                            eng.wait_ge(S["s_sev"], (b - 1) // 4 + 1)
                        for c in range(5):
                            lhs = PTt[:, 128 * c:128 * (c + 1)]
                            vs = (b + c) % NV
                            last = (c == 4 and b == 0)
                            eng.matmul(ps_x[:, 0:512], lhs,
                                       vwin[:, vs, 0:512],
                                       start=(c == 0), stop=last)
                            eng.matmul(ps_x[:, 512:1024], lhs,
                                       vwin[:, vs, 512:1024],
                                       start=(c == 0), stop=last)
                            eng.matmul(ps_a[:, 72:73], lhs, cones_col[:, :],
                                       start=(c == 0), stop=(c == 4))
                        if b >= 1:
                            eng.matmul(ps_x[:, 0:512], callones[0:b, :],
                                       S_sb[0:b, 0:512], start=False,
                                       stop=True)
                            eng.matmul(ps_x[:, 512:1024], callones[0:b, :],
                                       S_sb[0:b, 512:1024], start=False,
                                       stop=True)
                        eng.inc(S["s_pv"], 1)
                    if k is not None:
                        eng.wait_ge(S["s_xt"], 16 * (k + 1))
                        eng.wait_ge(S["s_wev"], cnt_b(it))
                        if k >= 1:
                            eng.wait_ge(S["s_shx"], k)
                            eng.wait_ge(S["s_zf"], cnt_b(it + 1))
                            eng.wait_ge(S["s_tpev"], 4 * max(g, 0))
                        p = k % 2
                        for dc in range(8):
                            lhs = xinT[:, k % 4, dc, :]
                            eng.matmul(ps_y[:, 0:512], lhs,
                                       wv_b[:, dc, 0:512],
                                       start=(dc == 0), stop=(dc == 7))
                            eng.matmul(ps_y[:, 512:1024], lhs,
                                       wv_b[:, dc, 512:1024],
                                       start=(dc == 0), stop=(dc == 7))
                            eng.matmul(ps_a[:, 0:65], lhs, wsc_b[:, dc, :],
                                       start=(dc == 0), stop=False)
                        eng.matmul(ps_a[:, 0:65], cones_row[:, :],
                                   bscr_b[:, :], start=False, stop=True)
                        eng.inc(S["s_vmm"], 1)
                    if (it - VLAG) % 4 == 0 and g >= 1:
                        eng.wait_ge(S["s_sev"], g)
                    if bz is not None:
                        eng.wait_ge(S["s_zh"], bz + 1)
                        row = 96 + ((it - VLAG) % 4)
                        st = g not in started
                        started[g] = True
                        lhs = cE[:, 155 - row:155 - row + 100]
                        last = grp_last[g] == ("z", it)
                        eng.matmul(ps_s[:, 0:512], lhs,
                                   zhat_sb[:, bz % 2, 0:512],
                                   start=st, stop=last)
                        eng.matmul(ps_s[:, 512:1024], lhs,
                                   zhat_sb[:, bz % 2, 512:1024],
                                   start=st, stop=last)
                        eng.inc(S["s_zhs"], 1)
                    if k is not None:
                        eng.wait_ge(S["s_vev"], k + 1)
                        st = g not in started
                        started[g] = True
                        lhs = cE[:, 155 - k:155 - k + 100]
                        last = grp_last[g] == ("s", it)
                        eng.matmul(ps_s[:, 0:512], lhs,
                                   vwin[:, k % NV, 0:512], start=st,
                                   stop=last)
                        eng.matmul(ps_s[:, 512:1024], lhs,
                                   vwin[:, k % NV, 512:1024], start=st,
                                   stop=last)
                        eng.inc(S["s_smm"], 1)
                    if k is not None:
                        eng.wait_ge(S["s_shx"], k + 1)
                        if k >= 1:
                            eng.wait_ge(S["s_shev"], k)
                        eng.transpose(ps_e[0:64, 128:256],
                                      shellU[:, k % 2, :], cidentf[:, :])
                        eng.inc(S["s_shT"], 1)
                        eng.wait_ge(S["s_shev"], k + 1)
                        if k >= 1:
                            eng.wait_ge(S["s_pjv"], k)
                        eng.matmul(ps_e[:, 0:32], shellUT[:, k % 2, :],
                                   omegaT_b[:, :], start=True, stop=True)
                        eng.inc(S["s_prj"], 1)
                    if b is not None:
                        eng.wait_ge(S["s_oat"], 16 * (b + 1))
                        eng.wait_ge(S["s_vev"], min(it - VLAG + 1, NKB))
                        for dc in range(8):
                            lhs = oaT[:, b % 2, dc, :]
                            eng.matmul(ps_y[:, 0:512], lhs,
                                       wout_b[:, dc, 0:512],
                                       start=(dc == 0), stop=False)
                            eng.matmul(ps_y[:, 512:1024], lhs,
                                       wout_b[:, dc, 512:1024],
                                       start=(dc == 0), stop=False)
                        eng.matmul(ps_y[:, 0:512], cones_row[:, :],
                                   cvec_b[0:1, 0:512], start=False,
                                   stop=True)
                        eng.matmul(ps_y[:, 512:1024], cones_row[:, :],
                                   cvec_b[0:1, 512:1024], start=False,
                                   stop=True)
                        eng.inc(S["s_wmm"], 1)
                    if grp_close(it):
                        ks = [kk for kk in range(4 * g, 4 * g + 4)
                              if kk < NKB]
                        for kb in ks:
                            eng.wait_ge(S["s_tp"], kb + 1)
                            if kb >= 1:
                                eng.wait_ge(S["s_tpev"], kb)
                            par = kb % 2
                            eng.transpose(ps_a[:, 128:256],
                                          tpt[:, par, 0:128], cidentf[:, :])
                            eng.transpose(ps_a[:, 256:384],
                                          tpt[:, par, 128:256],
                                          cidentf[:, :])
                            eng.transpose(ps_a[0:64, 384:512],
                                          tpt[:, par, 256:320],
                                          cidentf[:, :])
                            eng.inc(S["s_tpe"], 1)
                            eng.wait_ge(S["s_tpev"], kb + 1)
                            if kb >= 1:
                                eng.wait_ge(S["s_ptk"], kb)
                            for c in range(3):
                                npp = 128 if c < 2 else 64
                                eng.matmul(ps_e[:, 256:320],
                                           tpTs[0:npp, c, :],
                                           wcT_b[0:npp, c, :],
                                           start=(c == 0), stop=(c == 2))
                            eng.inc(S["s_phm"], 1)
                            eng.wait_ge(S["s_ptk"], kb + 1)
                            if kb >= 1:
                                eng.wait_ge(S["s_phi"], kb)
                            eng.transpose(ps_e[0:64, 320:448], phtk[:, :],
                                          cidentf[:, :])
                            eng.inc(S["s_ptT"], 1)
                    bs = it - QLAG + 1
                    if 0 <= bs < NQB:
                        eng.wait_ge(S["s_phi"], bs + 5)
                        eng.wait_ge(S["s_pvev"], bs)
                        qcol = 128 * (bs + 4)
                        for c in range(5):
                            kcol = 128 * (bs + c)
                            eng.matmul(ps_x[:, 128 * c:128 * (c + 1)],
                                       phiT[0:64, kcol:kcol + 128],
                                       phiT[0:64, qcol:qcol + 128],
                                       start=(c in (0, 4)),
                                       stop=(c in (3, 4)))
                        eng.inc(S["s_sc"], 1)

                eng.wait_ge(S["s_r"], 1)
                eng.matmul(ps_x[0:1, 0:512], cones_colf[0:4, :],
                           zsum4[:, 0:512], start=True, stop=True)
                eng.matmul(ps_x[0:1, 512:1024], cones_colf[0:4, :],
                           zsum4[:, 512:1024], start=True, stop=True)
                eng.inc(S["s_r"], 1)
                eng.wait_ge(S["s_rb"], 16 * 2)
                for c in range(8):
                    eng.matmul(ps_e[0:1, 384:392], zbnT[:, c:c + 1],
                               wr_b[:, c, :], start=(c == 0), stop=(c == 7))
                eng.inc(S["s_r"], 1)
                eng.wait_ge(S["s_r"], 9)
                eng.matmul(ps_e[:, 448:456], cones_rowf[:, :], r8b[:, :],
                           start=True, stop=True)
                eng.matmul(ps_e[:, 456:457], cones_rowf[:, :], r1a[:, :],
                           start=True, stop=True)
                eng.inc(S["s_r"], 1)
                eng.wait_ge(S["s_r"], 11)
                eng.wait_ge(S["s_rb"], 16 * 4)
                eng.matmul(ps_x[0:1, 0:512], ohc[:, :], bup8_b[:, 0:512],
                           start=True, stop=True)
                eng.matmul(ps_x[0:1, 512:1024], ohc[:, :],
                           bup8_b[:, 512:1024], start=True, stop=True)
                eng.matmul(ps_e[0:1, 384:448], ohc[:, :], bdown8_b[:, :],
                           start=True, stop=False)
                for dc in range(8):
                    eng.matmul(ps_e[0:1, 384:448], lnbT_b[:, dc:dc + 1],
                               weff[:, 1, dc, :], start=False,
                               stop=(dc == 7))
                eng.inc(S["s_r"], 1)

                eng.wait_ge(S["s_r"], 14)
                eng.wait_ge(S["s_rb"], 16 * 6)
                for j in range(NQB):
                    eng.wait_ge(S["s_zt"], 16 * (j + 1))
                    if j >= 1:
                        eng.wait_ge(S["s_hev"], j)
                    for dc in range(8):
                        eng.matmul(ps_a[0:64, 0:128], wdg_b[:, dc, :],
                                   zhT2[:, j % 2, dc, :],
                                   start=(dc == 0), stop=(dc == 7))
                    eng.inc(S["s_h"], 1)
                    py = ps_x if j % 2 == 0 else ps_y
                    eng.wait_ge(S["s_hev"], j + 1)
                    if j >= 2:
                        eng.wait_ge(S["s_tg"], j - 1)
                    eng.matmul(py[:, 0:512], hTt[:, j % 2, :],
                               wupe[0:64, 0, 0:512], start=True,
                               stop=False)
                    eng.matmul(py[:, 512:1024], hTt[:, j % 2, :],
                               wupe[0:64, 0, 512:1024], start=True,
                               stop=False)
                    eng.matmul(py[:, 0:512], cones_row[:, :],
                               bup_eb[0:1, 0:512], start=False, stop=True)
                    eng.matmul(py[:, 512:1024], cones_row[:, :],
                               bup_eb[0:1, 512:1024], start=False,
                               stop=True)
                    eng.inc(S["s_ymm"], 1)

            @block.vector
            def _(eng):
                eng = _Rec(eng)
                emit_setup(eng, "vector")
                setup_barrier(eng)

                dvec = [0]

                def dsync():
                    eng.inc(S["s_dve"], 1)
                    dvec[0] += 1
                    eng.wait_ge(S["s_dve"], dvec[0])

                def ln_stats(src_ap, mv, mu_o, rv_o, p):
                    eng.bn_stats(mv[:, p, 0:6], src_ap[:, 0:512])
                    eng.bn_stats(mv[:, p, 6:12], src_ap[:, 512:1024])
                    eng.bn_aggr(mv[:, p, 12:14], mv[:, p, 0:6])
                    eng.bn_aggr(mv[:, p, 14:16], mv[:, p, 6:12])
                    dsync()
                    eng.tensor_scalar(mu_o[:, p, :], mv[:, p, 12:13], 0.5,
                                      None, ALU.mult)
                    eng.tensor_tensor(mv[:, p, 0:1], mv[:, p, 12:13],
                                      mv[:, p, 14:15], ALU.subtract)
                    eng.tensor_tensor(rv_o[:, p, :], mv[:, p, 13:14],
                                      mv[:, p, 15:16], ALU.add)
                    dsync()
                    eng.scalar_tensor_tensor(mu_o[:, p, :], mv[:, p, 14:15],
                                             0.5, mu_o[:, p, :], ALU.mult,
                                             ALU.add)
                    eng.tensor_scalar(mv[:, p, 0:1], mv[:, p, 0:1], 0.5,
                                      None, ALU.mult)
                    eng.tensor_scalar(rv_o[:, p, :], rv_o[:, p, :], 0.5,
                                      EPS, ALU.mult, ALU.add)
                    dsync()
                    eng.tensor_tensor(mv[:, p, 1:2], mv[:, p, 0:1],
                                      mv[:, p, 0:1], ALU.mult)
                    dsync()
                    eng.tensor_tensor(rv_o[:, p, :], rv_o[:, p, :],
                                      mv[:, p, 1:2], ALU.add)

                def sinpoly(dst_ap, x_ap, y_ap, t_ap):
                    a = list(reversed(SIN_C[1:]))
                    eng.tensor_scalar(t_ap, y_ap, a[0], None, ALU.mult)
                    for cc in a[1:]:
                        eng.scalar_tensor_tensor(t_ap, t_ap, cc, y_ap,
                                                 ALU.add, ALU.mult)
                    eng.scalar_tensor_tensor(dst_ap, t_ap, SIN_C[0], x_ap,
                                             ALU.add, ALU.mult)

                for it in range(NITER):
                    kf, b, bz = kv(it), bvf(it), bzf(it)
                    k = k2v(it)
                    bw = bwf(it)
                    g = g2f(it)
                    if kf is not None:
                        eng.wait_ge(S["s_x"], 16 * (kf + 1))
                        ln_stats(xbuf[:, kf % 3, :], mvk, muk, rvk, kf % 2)
                        eng.inc(S["s_st0"], 1)
                    if b is not None:
                        eng.wait_ge(S["s_sc"], b + 1)
                        if b >= 1:
                            eng.wait_ge(S["s_exp"], b)
                        eng.scalar_tensor_tensor(scf[:, :], ps_x[:, 0:640],
                                                 crtemp[:, 0:1], cTWT[:, :],
                                                 ALU.mult, ALU.mult)
                        eng.inc(S["s_pst"], 1)
                        eng.wait_ge(S["s_exp"], b + 1)
                        eng.tensor_tensor(PTt[:, 512:640], PTt[:, 512:640],
                                          cmaskT[:, :], ALU.mult)
                        eng.inc(S["s_msk"], 1)
                        eng.wait_ge(S["s_pv"], b + 1)
                        p = b % 2
                        coeff = float(max(0, 128 * b - 512)
                                      - max(0, 512 - 128 * b))
                        eng.tensor_scalar(zqb[:, p, :], ps_a[:, 72:73],
                                          coeff, None, ALU.add)
                        dsync()
                        eng.reciprocal(rZb[:, p, :], zqb[:, p, :])
                        eng.inc(S["s_zf"], 1)
                    if k is not None:
                        p = k % 2
                        eng.wait_ge(S["s_sqr"], k + 1)
                        eng.tensor_scalar(n2cb[:, p, :], n2cb[:, p, :],
                                          1e-24, None, ALU.max)
                        eng.inc(S["s_n2c"], 1)
                        vfac = 0.0 if k < 4 else 1.0
                        eng.tensor_scalar(pcs[:, p, :], e2q[:, p, :],
                                          1.0, None, ALU.add)
                        dsync()
                        eng.reciprocal(qvv[:, p, :], pcs[:, p, :])
                        dsync()
                        eng.tensor_scalar(qvv[:, p, :], qvv[:, p, :], -2.0,
                                          1.0, ALU.mult, ALU.add)
                        sl = k % 4
                        rs = RFF_SCALE
                        eng.memset(pc4[:, sl, 0:1], MAGS[0] * rs * vfac)
                        dsync()
                        eng.tensor_scalar(pc4[:, sl, 1:2], qvv[:, p, :],
                                          MAGS[1] * rs * vfac, None,
                                          ALU.mult)
                        eng.tensor_tensor(pc4[:, sl, 2:3], qvv[:, p, :],
                                          qvv[:, p, :], ALU.mult)
                        dsync()
                        eng.tensor_scalar(pc4[:, sl, 3:4], pc4[:, sl, 2:3],
                                          MAGS[3] * rs * vfac, None,
                                          ALU.mult)
                        eng.tensor_tensor(pc4[:, sl, 4:5], pc4[:, sl, 2:3],
                                          pc4[:, sl, 2:3], ALU.mult)
                        eng.tensor_scalar(pc4[:, sl, 2:3], pc4[:, sl, 2:3],
                                          MAGS[2] * rs * vfac, None,
                                          ALU.mult)
                        dsync()
                        eng.tensor_tensor(pc4[:, sl, 3:4], pc4[:, sl, 3:4],
                                          qvv[:, p, :], ALU.mult)
                        eng.tensor_scalar(pc4[:, sl, 4:5], pc4[:, sl, 4:5],
                                          MAGS[4] * rs * vfac, None,
                                          ALU.mult)
                    if k is not None:
                        eng.wait_ge(S["s_shT"], k + 1)
                        if k >= 1:
                            eng.wait_ge(S["s_prj"], k)
                        eng.tensor_copy(shellUT[:, k % 2, :],
                                        ps_e[0:64, 128:256])
                        eng.inc(S["s_shev"], 1)
                        eng.wait_ge(S["s_prj"], k + 1)
                        eng.tensor_copy(proj4s[:, k % 4, :], ps_e[:, 0:32])
                        eng.inc(S["s_pjv"], 1)
                    if grp_close(it):
                        ks = [kk for kk in range(4 * g, 4 * g + 4)
                              if kk < NKB]
                        if ks:
                            eng.wait_ge(S["s_pjv"], ks[-1] + 1)
                            nblk = len(ks)
                            w = 32 * nblk
                            pr3 = proj4s[:, 0:nblk, :]
                            eng.scalar_tensor_tensor(
                                rrX[:, 0:w],
                                pr3.rearrange("p a b -> p (a b)"), 1.0,
                                crff[:, 0:w], ALU.mult, ALU.add)
                            eng.tensor_scalar(rrY[:, 0:w], rrX[:, 0:w],
                                              1.0 / (2 * PI), None, ALU.mult)
                            eng.tensor_scalar(rrH[:, 0:w], rrY[:, 0:w],
                                              0.5, None, ALU.is_ge)
                            for j in range(1, 5):
                                if w < 64:
                                    dsync()
                                eng.scalar_tensor_tensor(
                                    rrH[:, 0:w], rrY[:, 0:w], j + 0.5,
                                    rrH[:, 0:w], ALU.is_ge, ALU.add)
                            eng.tensor_scalar(rrT[:, 0:w], rrY[:, 0:w],
                                              -0.5, None, ALU.is_le)
                            for j in range(1, 5):
                                if w < 64:
                                    dsync()
                                eng.scalar_tensor_tensor(
                                    rrT[:, 0:w], rrY[:, 0:w], -(j + 0.5),
                                    rrT[:, 0:w], ALU.is_le, ALU.add)
                            if w < 64:
                                dsync()
                            eng.tensor_tensor(rrH[:, 0:w], rrH[:, 0:w],
                                              rrT[:, 0:w], ALU.subtract)
                            if w < 64:
                                dsync()
                            eng.scalar_tensor_tensor(rrY[:, 0:w],
                                                     rrH[:, 0:w], -2 * PI,
                                                     rrX[:, 0:w], ALU.mult,
                                                     ALU.add)
                            eng.tensor_tensor(rrH[:, 0:w], rrY[:, 0:w],
                                              rrY[:, 0:w], ALU.mult)
                            sin_dst = phis4[:, 0:nblk, 32:64]
                            sinpoly(sin_dst, rrY[:, 0:w]
                                    .rearrange("p (a b) -> p a b", b=32),
                                    rrH[:, 0:w]
                                    .rearrange("p (a b) -> p a b", b=32),
                                    rrT[:, 0:w]
                                    .rearrange("p (a b) -> p a b", b=32))
                            eng.tensor_scalar(rrT[:, 0:w], rrY[:, 0:w],
                                              PI / 2, None, ALU.is_ge)
                            if w < 64:
                                dsync()
                            eng.scalar_tensor_tensor(rrY[:, 0:w],
                                                     rrT[:, 0:w], -2 * PI,
                                                     rrY[:, 0:w], ALU.mult,
                                                     ALU.add)
                            eng.tensor_scalar(rrY[:, 0:w], rrY[:, 0:w],
                                              PI / 2, None, ALU.add)
                            eng.tensor_tensor(rrH[:, 0:w], rrY[:, 0:w],
                                              rrY[:, 0:w], ALU.mult)
                            cos_dst = phis4[:, 0:nblk, 0:32]
                            sinpoly(cos_dst, rrY[:, 0:w]
                                    .rearrange("p (a b) -> p a b", b=32),
                                    rrH[:, 0:w]
                                    .rearrange("p (a b) -> p a b", b=32),
                                    rrT[:, 0:w]
                                    .rearrange("p (a b) -> p a b", b=32))
                            for kb in ks:
                                sl = kb % 4
                                par = kb % 2
                                if kb >= 2:
                                    eng.wait_ge(S["s_tpe"], kb - 1)
                                for m in range(5):
                                    eng.tensor_scalar(
                                        tpt[:, par, 64 * m:64 * (m + 1)],
                                        phis4[:, sl, :],
                                        pc4[:, sl, m:m + 1], None,
                                        ALU.mult)
                                eng.inc(S["s_tp"], 1)
                                eng.wait_ge(S["s_tpe"], kb + 1)
                                eng.tensor_copy(tpTs[:, 0, :],
                                                ps_a[:, 128:256])
                                eng.tensor_copy(tpTs[:, 1, :],
                                                ps_a[:, 256:384])
                                eng.tensor_copy(tpTs[0:64, 2, :],
                                                ps_a[0:64, 384:512])
                                eng.inc(S["s_tpev"], 1)
                                eng.wait_ge(S["s_phm"], kb + 1)
                                eng.tensor_copy(phtk[:, :],
                                                ps_e[:, 256:320])
                                eng.inc(S["s_ptk"], 1)
                                eng.wait_ge(S["s_ptT"], kb + 1)
                                eng.tensor_copy(
                                    phiT[0:64, 128 * kb:128 * (kb + 1)],
                                    ps_e[0:64, 320:448])
                                eng.inc(S["s_phi"], 1)
                    if bw is not None:
                        eng.wait_ge(S["s_wev"], bw + 1)
                        ln_stats(tb2, mva, mua, rva, bw % 2)
                        eng.inc(S["s_ast"], 1)
                        eng.wait_ge(S["s_t2e"], bw + 1)
                        eng.wait_ge(S["s_xq"], 16 * (bw + 1))
                        eng.tensor_add(y1buf[:, :], t2buf[:, :],
                                       xqbuf[:, bw % 2, :])
                        eng.inc(S["s_y1"], 1)
                        ln_stats(y1buf, mv1, mu1, rv1, bw % 2)
                        eng.inc(S["s_st1"], 1)
                    if grp_close(it):
                        ks = [kk for kk in range(4 * g, 4 * g + 4)
                              if kk < NKB]
                        if ks:
                            eng.wait_ge(S["s_smm"], smm_cnt[it])
                        eng.wait_ge(S["s_zhs"], zhs_cnt[it])
                        bzs = [bzf(i2) for i2 in
                               range(4 * g + VLAG,
                                     min(4 * g + 4 + VLAG, NITER))
                               if bzf(i2) is not None]
                        if ks:
                            eng.tensor_tensor(S_sb[:, :], S_sb[:, :],
                                              ps_s[0:72, :], ALU.add)
                        if bzs:
                            eng.tensor_tensor(zsum4[0:4, :], zsum4[0:4, :],
                                              ps_s[96:100, :], ALU.add)
                        eng.inc(S["s_sev"], 1)

                eng.wait_ge(S["s_sev"], NGRP)
                eng.wait_ge(S["s_wl"], 16 * 7)
                eng.inc(S["s_r"], 1)
                eng.wait_ge(S["s_r"], 2)
                eng.tensor_scalar(stage[0:1, 0:1024], ps_x[0:1, 0:1024],
                                  1.0 / SEQ, None, ALU.mult)
                eng.tensor_tensor(stage[0:1, 0:1024], stage[0:1, 0:1024],
                                  xbuf[0:1, 0, 0:1024], ALU.mult)
                eng.tensor_tensor(stage[0:1, 0:1024], stage[0:1, 0:1024],
                                  xbuf[0:1, 1, 0:1024], ALU.add)
                eng.tensor_scalar(tb2[0:1, 0:1024], y1buf[0:1, 0:1024],
                                  EPS, None, ALU.add)
                eng.inc(S["s_r"], 1)
                eng.wait_ge(S["s_r"], 4)
                eng.tensor_tensor(tb2[0:1, 0:1024], tb2[0:1, 0:1024],
                                  xqbuf[0:1, 0, 0:1024], ALU.mult)
                eng.tensor_tensor(stage[0:1, 0:1024], stage[0:1, 0:1024],
                                  xbuf[0:1, 2, 0:1024], ALU.subtract)
                eng.tensor_tensor(stage[0:1, 0:1024], stage[0:1, 0:1024],
                                  tb2[0:1, 0:1024], ALU.mult)
                eng.tensor_tensor(stage[0:1, 0:1024], stage[0:1, 0:1024],
                                  xqbuf[0:1, 1, 0:1024], ALU.add)
                eng.inc(S["s_r"], 1)
                eng.wait_ge(S["s_r"], 6)
                eng.tensor_add(r8a[:, :], ps_e[0:1, 384:392],
                               stage2[0:1, 0:8])
                dsync()
                eng.tensor_reduce(r1a[:, :], r8a[:, :],
                                  mybir.AxisListType.X, ALU.max)
                dsync()
                eng.tensor_scalar(r1b[:, :], r1a[:, :], -1.0, None,
                                  ALU.mult)
                eng.inc(S["s_r"], 1)
                eng.wait_ge(S["s_r"], 8)
                eng.tensor_reduce(r1a[:, :], r8a[:, :],
                                  mybir.AxisListType.X, ALU.add)
                dsync()
                eng.reciprocal(r1b[:, :], r1a[:, :])
                dsync()
                eng.tensor_tensor(r8b[:, :], r8a[:, :],
                                  r1b[0:1, 0:1].to_broadcast([1, 8]),
                                  ALU.mult)
                dsync()
                eng.tensor_reduce(r1a[:, :], r8b[:, :],
                                  mybir.AxisListType.X, ALU.max)
                dsync()
                eng.tensor_scalar(r8c[:, :], r8b[:, :], r1a[0:1, 0:1],
                                  None, ALU.is_ge)
                dsync()
                eng.tensor_tensor(r8c[:, :], r8c[:, :], cwpri[:, :],
                                  ALU.mult)
                dsync()
                eng.tensor_reduce(r1b[:, :], r8c[:, :],
                                  mybir.AxisListType.X, ALU.max)
                dsync()
                eng.tensor_scalar(r8b[:, :], r8c[:, :], r1b[0:1, 0:1],
                                  None, ALU.is_ge)
                dsync()
                eng.tensor_copy(stage2[0:1, 16:24], r8b[:, :])
                eng.inc(S["s_r"], 1)
                eng.wait_ge(S["s_r"], 10)
                eng.tensor_copy(ohb[:, :], ps_e[:, 448:456])
                eng.tensor_copy(gcol[:, :], ps_e[:, 456:457])
                for e8 in range(8):
                    src = wdown_b[:, :, e8, :]
                    if e8 == 0:
                        eng.tensor_scalar(weff[:, 0, :, :], src,
                                          ohb[:, 0:1], None, ALU.mult)
                    else:
                        eng.scalar_tensor_tensor(
                            weff[:, e8 % 2, :, :], src, ohb[:, e8:e8 + 1],
                            weff[:, 1 - (e8 % 2), :, :], ALU.mult, ALU.add)
                for dc in range(8):
                    eng.tensor_scalar(wdg_b[:, dc, :], weff[:, 1, dc, :],
                                      gTc[:, dc:dc + 1], None, ALU.mult)
                for e8 in range(8):
                    src = wup_b[0:64, e8, :]
                    if e8 == 0:
                        eng.tensor_scalar(wupe[0:64, 0, :], src,
                                          ohb[0:64, 0:1], None, ALU.mult)
                    else:
                        eng.scalar_tensor_tensor(
                            wupe[0:64, e8 % 2, :], src,
                            ohb[0:64, e8:e8 + 1],
                            wupe[0:64, 1 - (e8 % 2), :], ALU.mult, ALU.add)
                eng.tensor_copy(wupe[0:64, 0, :], wupe[0:64, 1, :])
                eng.inc(S["s_r"], 1)
                eng.wait_ge(S["s_r"], 12)
                eng.tensor_copy(bup_eb[:, :], ps_x[0:1, 0:1024])
                eng.tensor_copy(stage2[0:1, 32:96], ps_e[0:1, 384:448])
                eng.inc(S["s_r"], 1)
                eng.nop()
                eng.inc(S["s_r"], 1)

                for j in range(NQB):
                    eng.wait_ge(S["s_tg"], j + 1)
                    ln_stats(tb2, mvy, muy, rvy, j % 2)
                    eng.inc(S["s_yst"], 1)
                    eng.wait_ge(S["s_zld"], 16 * (j + 1))
                    eng.tensor_scalar(y1buf[:, :], zhat_sb[:, j % 2, :],
                                      stats1[:, j, 1:2], stats1[:, j, 0:1],
                                      ALU.mult, ALU.add)
                    eng.wait_ge(S["s_t3"], j + 1)
                    if j >= 2:
                        eng.wait_ge(S["s_out"], 16 * (j - 1))
                    eng.tensor_add(y2buf[:, j % 2, :], t2buf[:, :],
                                   y1buf[:, :])
                    eng.inc(S["s_y2"], 1)

            @block.scalar
            def _(eng):
                eng = _Rec(eng)
                emit_setup(eng, "scalar")
                setup_barrier(eng)

                acdc = [0]

                def adrain():
                    eng.inc(S["s_acd"], 1)
                    acdc[0] += 1
                    eng.wait_ge(S["s_acd"], acdc[0])

                for it in range(NITER):
                    kf, b = kv(it), bvf(it)
                    k = k2v(it)
                    bw = bwf(it)
                    if kf is not None:
                        p = kf % 2
                        eng.wait_ge(S["s_st0"], kf + 1)
                        eng.wait_ge(S["s_xt"], 16 * kf)
                        if kf >= 4:
                            eng.wait_ge(S["s_vmm"], kf - 3)
                        eng.activation(rstdk[:, p, :], rvk[:, p, :], AF.Ln)
                        eng.activation(rstdk[:, p, :], rstdk[:, p, :],
                                       AF.Exp, scale=-0.5)
                        eng.mul(nmrk[:, p, :], muk[:, p, :], -1.0)
                        eng.mul(nmrk[:, p, :], nmrk[:, p, :],
                                rstdk[:, p, :])
                        adrain()
                        eng.activation(xinb[:, p, :], xbuf[:, kf % 3, :],
                                       AF.Identity, bias=nmrk[:, p, :],
                                       scale=rstdk[:, p, :]
                                       ).then_inc(S["s_ln"], 1)
                        eng.wait_ge(S["s_ln"], kf + 1)
                        eng.dma_start(out=xinT[:, kf % 4, :, :],
                                      in_=xinb[:, p, :]
                                      .rearrange("p (c f) -> p c f", f=128),
                                      transpose=True).then_inc(S["s_xt"], 16)
                    if b is not None:
                        eng.wait_ge(S["s_pst"], b + 1)
                        if b >= 1:
                            eng.wait_ge(S["s_pv"], b)
                        eng.activation(PTt[:, :], scf[:, :], AF.Exp
                                       ).then_inc(S["s_exp"], 1)
                    if k is not None:
                        p = k % 2
                        eng.wait_ge(S["s_vmm"], k + 1)
                        if k >= NV:
                            eng.wait_ge(S["s_pv"], k - NV + 1)
                        eng.activation(vwin[:, k % NV, :], ps_y[:, :],
                                       AF.Copy).then_inc(S["s_vev"], 1)
                        eng.activation(sqs[:, :], ps_a[:, 0:64], AF.Square,
                                       accum_out=n2cb[:, p, :])
                        eng.activation(e2q[:, p, :], ps_a[:, 64:65], AF.Exp,
                                       scale=2.0).then_inc(S["s_sqr"], 1)
                        eng.wait_ge(S["s_n2c"], k + 1)
                        eng.activation(rnorm[:, p, :], n2cb[:, p, :], AF.Ln)
                        eng.activation(rnorm[:, p, :], rnorm[:, p, :],
                                       AF.Exp, scale=-0.5)
                        adrain()
                        if k >= 1:
                            eng.wait_ge(S["s_shT"], k)
                        eng.activation(shellU[:, p, :], ps_a[:, 0:64],
                                       AF.Identity, scale=rnorm[:, p, :]
                                       ).then_inc(S["s_shx"], 1)
                    if b is not None:
                        p = b % 2
                        eng.wait_ge(S["s_zf"], b + 1)
                        eng.wait_ge(S["s_oat"], 16 * b)
                        eng.activation(oa_bf[:, p, :], ps_x[:, :],
                                       AF.Identity, scale=rZb[:, p, :]
                                       ).then_inc(S["s_pvev"], 1)
                        eng.wait_ge(S["s_pvev"], b + 1)
                        if b >= 1:
                            eng.wait_ge(S["s_wmm"], b)
                        eng.dma_start(out=oaT[:, p, :, :],
                                      in_=oa_bf[:, p, :]
                                      .rearrange("p (c f) -> p c f", f=128),
                                      transpose=True).then_inc(S["s_oat"], 16)
                    if bw is not None:
                        p = bw % 2
                        eng.wait_ge(S["s_ast"], bw + 1)
                        eng.activation(rstda[:, p, :], rva[:, p, :], AF.Ln)
                        eng.activation(rstda[:, p, :], rstda[:, p, :],
                                       AF.Exp, scale=-0.5)
                        eng.mul(nmra[:, p, :], mua[:, p, :], -1.0)
                        eng.mul(nmra[:, p, :], nmra[:, p, :],
                                rstda[:, p, :])
                        adrain()
                        eng.activation(t2buf[:, :], tb2[:, :], AF.Identity,
                                       bias=nmra[:, p, :],
                                       scale=rstda[:, p, :]
                                       ).then_inc(S["s_t2e"], 1)
                        eng.wait_ge(S["s_st1"], bw + 1)
                        if bw >= 2:
                            eng.wait_ge(S["s_zhs"], bw - 1)
                            eng.wait_ge(S["s_zsc"], 16 * (bw - 1))
                        eng.activation(r1cb[:, p, :], rv1[:, p, :], AF.Ln)
                        eng.activation(sg1[:, p, :], r1cb[:, p, :], AF.Exp,
                                       scale=0.5)
                        eng.activation(r1cb[:, p, :], r1cb[:, p, :], AF.Exp,
                                       scale=-0.5)
                        eng.mul(nmr1[:, p, :], mu1[:, p, :], -1.0)
                        eng.mul(nmr1[:, p, :], nmr1[:, p, :], r1cb[:, p, :])
                        eng.copy(stats1[:, bw, 0:1], mu1[:, p, :])
                        eng.copy(stats1[:, bw, 1:2], sg1[:, p, :])
                        adrain()
                        eng.activation(zhat_sb[:, p, :], y1buf[:, :],
                                       AF.Identity, bias=nmr1[:, p, :],
                                       scale=r1cb[:, p, :]
                                       ).then_inc(S["s_zh"], 1)
                    if b is not None:
                        if b >= 1:
                            eng.wait_ge(S["s_ast"], b)
                        eng.wait_ge(S["s_wmm"], b + 1)
                        eng.activation(tb2[:, :], ps_y[:, :], AF.Copy
                                       ).then_inc(S["s_wev"], 1)

                eng.wait_ge(S["s_r"], 3)
                eng.activation(tb2[0:1, 0:1024], tb2[0:1, 0:1024], AF.Ln)
                eng.activation(tb2[0:1, 0:1024], tb2[0:1, 0:1024], AF.Exp,
                               scale=-0.5).then_inc(S["s_r"], 1)
                eng.wait_ge(S["s_r"], 7)
                eng.activation(r8a[:, :], r8a[:, :], AF.Exp,
                               bias=r1b[0:1, 0:1]).then_inc(S["s_r"], 1)

                eng.wait_ge(S["s_r"], 14)
                eng.wait_ge(S["s_rb"], 16 * 6)
                for j in range(NQB):
                    py = ps_x if j % 2 == 0 else ps_y
                    eng.wait_ge(S["s_h"], j + 1)
                    eng.activation(hTt[:, j % 2, :], ps_a[0:64, 0:128],
                                   AF.Relu, bias=bdcol[:, :]
                                   ).then_inc(S["s_hev"], 1)
                    eng.wait_ge(S["s_ymm"], j + 1)
                    eng.activation(tb2[:, :], py[:, :], AF.Identity,
                                   scale=gcol[:, :]).then_inc(S["s_tg"], 1)
                    eng.wait_ge(S["s_yst"], j + 1)
                    pj = j % 2
                    eng.activation(rYc[:, pj, :], rvy[:, pj, :], AF.Ln)
                    eng.activation(rYc[:, pj, :], rYc[:, pj, :], AF.Exp,
                                   scale=-0.5)
                    eng.mul(nmry[:, pj, :], muy[:, pj, :], -1.0)
                    eng.mul(nmry[:, pj, :], nmry[:, pj, :], rYc[:, pj, :])
                    adrain()
                    eng.activation(t2buf[:, :], tb2[:, :], AF.Identity,
                                   bias=nmry[:, pj, :],
                                   scale=rYc[:, pj, :]
                                   ).then_inc(S["s_t3"], 1)

            @block.gpsimd
            def _(eng):
                eng = _Rec(eng)
                setup_barrier(eng)
                for b in range(NQB):
                    eng.wait_ge(S["s_zh"], b + 1)
                    eng.wait_ge(S["s_zsc"], 16 * b)
                    eng.dma_start(out=zhat_dram[128 * b:128 * (b + 1), :],
                                  in_=zhat_sb[:, b % 2, :]
                                  ).then_inc(S["s_zsc"], 16)
                eng.wait_ge(S["s_r"], 5)
                eng.dma_start(out=rb_dram[0:1, 0:1024],
                              in_=stage[0:1, 0:1024]).then_inc(S["s_rb"], 16)
                eng.wait_ge(S["s_rb"], 16 * 1)
                eng.dma_start(out=zbnT[:, :],
                              in_=rb_dram[0:1, :]
                              .rearrange("q (c p) -> (q p) c", p=128),
                              allow_slow_non_contiguous=True
                              ).then_inc(S["s_rb"], 16)
                eng.wait_ge(S["s_r"], 9)
                eng.wait_ge(S["s_rb"], 16 * 2)
                eng.dma_start(out=rb_dram[0:1, 0:8],
                              in_=stage2[0:1, 16:24]).then_inc(S["s_rb"], 16)
                eng.wait_ge(S["s_rb"], 16 * 3)
                eng.dma_start(out=ohc[:, :],
                              in_=rb_dram[0:1, 0:8]
                              .rearrange("q (c o) -> (q c) o", o=1)
                              ).then_inc(S["s_rb"], 16)
                eng.wait_ge(S["s_r"], 13)
                eng.wait_ge(S["s_rb"], 16 * 4)
                eng.dma_start(out=rb_dram[0:1, 0:64],
                              in_=stage2[0:1, 32:96]).then_inc(S["s_rb"], 16)
                eng.wait_ge(S["s_rb"], 16 * 5)
                eng.dma_start(out=bdcol[:, :],
                              in_=rb_dram[0:1, 0:64]
                              .rearrange("q (c o) -> (q c) o", o=1)
                              ).then_inc(S["s_rb"], 16)
                for j in range(NQB):
                    eng.wait_ge(S["s_y2"], j + 1)
                    eng.wait_ge(S["s_out"], 16 * j)
                    eng.dma_start(out=y_out[128 * j:128 * (j + 1), :],
                                  in_=y2buf[:, j % 2, :]
                                  ).then_inc(S["s_out"], 16)
                eng.wait_ge(S["s_out"], 16 * NQB)
                if cfg.get("DEBUG"):
                    nrb = [0]
                    for nm, ap in dbg_list:
                        eng.wait_ge(S["s_rb"], 16 * (5 + nrb[0]))
                        eng.dma_start(out=dbg_outs[nm], in_=ap
                                      ).then_inc(S["s_rb"], 16)
                        nrb[0] += 1

    return nc



_NC_CACHE = {}


def _get_nc(seq_len):
    if seq_len not in _NC_CACHE:
        _NC_CACHE[seq_len] = build_nc(make_cfg(seq_len))
    return _NC_CACHE[seq_len]


def kernel(**inputs):
    x = np.asarray(inputs["L0_attn_x"])
    B, N, D = x.shape
    cfg = make_cfg(N)
    consts = host_constants(cfg)
    ws = stage_weights({k: np.asarray(v) for k, v in inputs.items()})
    nc = _get_nc(N)

    base = {}
    for nm, (shape, dty) in IN_SPECS.items():
        src = ws.get(nm)
        if src is None:
            src = consts[nm]
        arr = np.asarray(src)
        if dty == F32:
            arr = np.ascontiguousarray(arr, dtype=np.float32)
        else:
            arr = np.ascontiguousarray(arr, dtype=ml_dtypes.bfloat16)
        assert list(arr.shape) == shape, (nm, arr.shape, shape)
        base[nm] = arr

    pad = np.zeros((512, D), np.float32)
    in_maps = []
    for c in range(8):
        seq = c // 2
        xk = np.ascontiguousarray(
            np.concatenate([pad, np.asarray(x[seq], np.float32)], axis=0))
        m = dict(base)
        m["x_k"] = xk
        in_maps.append(m)

    res = run_bass_kernel_spmd(nc, in_maps, core_ids=list(range(8)))
    out = np.empty((B, N, D), np.float32)
    for seq in range(B):
        out[seq] = res.results[2 * seq]["y_out"]
    return out


# version 10
# speedup vs baseline: 1.0821x; 1.0821x over previous
import contextlib
import math
import numpy as np
import ml_dtypes
import concourse.bass as bass
import concourse.mybir as mybir
from concourse.bass_utils import run_bass_kernel_spmd

F32 = mybir.dt.float32
BF16 = mybir.dt.bfloat16
AF = mybir.ActivationFunctionType
ALU = mybir.AluOpType

PI = float(np.pi)
MAGIC = 12582912.0
EPS = 1e-5
DECAY = 0.03
RFF_SCALE = math.sqrt(2.0 / 64.0) * math.sqrt(math.exp(-0.5))
MAGS = [math.sqrt(abs((-1.0) ** m / math.factorial(m)) + 1e-10)
        for m in range(5)]


def _trig_coeffs():
    xs = np.cos((np.arange(4000) + 0.5) * np.pi / 4000) * np.pi
    A = np.stack([xs ** k for k in range(14)], 1)
    cs, *_ = np.linalg.lstsq(A, np.sin(xs), rcond=None)
    return [float(cs[k]) for k in (1, 3, 5, 7, 9, 11)]


SIN_C = _trig_coeffs()


def make_cfg(seq_len):
    nqb = seq_len // 128
    return dict(SEQ=seq_len, NQB=nqb, NKB=nqb + 4, NV=12, QLAG=10, VLAG=2,
                NITER=nqb + 12)


def host_constants(cfg):
    jw = (np.arange(5)[:, None, None] * 128 + np.arange(128)[None, :, None])
    i = np.arange(128)[None, None, :]
    dist = 512 + i - jw
    twt = np.where(dist >= 0, np.exp(-DECAY * np.maximum(dist, 0.0)), 0.0)
    TWT = np.ascontiguousarray(
        twt.transpose(1, 0, 2).reshape(128, 640)).astype(np.float32)
    maskT = (np.arange(128)[:, None] <= np.arange(128)[None, :]
             ).astype(ml_dtypes.bfloat16)
    E = np.zeros((128, 256), ml_dtypes.bfloat16)
    E[:, 155] = 1
    return dict(
        TWT=TWT, maskT=maskT, E=E,
        ident=np.eye(128, dtype=ml_dtypes.bfloat16),
        identf=np.eye(128, dtype=np.float32),
        allones=np.ones((128, 128), ml_dtypes.bfloat16),
        ones_row=np.ones((1, 128), ml_dtypes.bfloat16),
        ones_rowf=np.ones((1, 128), np.float32),
        ones_colf=np.ones((128, 1), np.float32),
        ones_col=np.ones((128, 1), ml_dtypes.bfloat16),
        wpri=np.arange(8, 0, -1, dtype=np.float32)[None, :],
    )


def stage_weights(w):
    out = {}
    out["wv"] = np.ascontiguousarray(w["L0_attn_W_v"].T)
    out["wout"] = np.ascontiguousarray(w["L0_attn_W_out"].T)
    out["wsc"] = np.ascontiguousarray(np.concatenate(
        [w["L0_attn_W_shell"].T, w["L0_attn_W_charge"].T], axis=1))
    out["bsc_row"] = np.ascontiguousarray(np.concatenate(
        [w["L0_attn_b_shell"], w["L0_attn_b_charge"]])[None, :])
    out["omegaT"] = np.ascontiguousarray(w["L0_attn_omega"].T)
    wc = w["L0_attn_W_combine"]
    wcT = np.zeros((128, 3, 64), np.float32)
    for c in range(3):
        n = min(128, 320 - 128 * c)
        wcT[:n, c, :] = wc[:, 128 * c:128 * c + n].T
    out["wcT"] = np.ascontiguousarray(wcT.reshape(128, 192))
    out["rff4"] = np.ascontiguousarray(np.repeat(
        np.tile(np.asarray(w["L0_attn_rff_bias"], np.float32), 4)[None, :],
        128, axis=0))
    out["b_v_T"] = np.ascontiguousarray(
        np.asarray(w["L0_attn_b_v"], np.float32).reshape(8, 128).T)
    out["b_out_row"] = np.ascontiguousarray(w["L0_attn_b_out"][None, :])
    out["log_temp"] = np.full((1, 1), float(w["L0_attn_log_temp"]), np.float32)
    out["gT"] = np.ascontiguousarray(
        np.asarray(w["L0_ffn_ln_g"], np.float32).reshape(8, 128).T)
    out["lnbT"] = np.ascontiguousarray(
        np.asarray(w["L0_ffn_ln_b"], np.float32).reshape(8, 128).T)
    out["ln_g_row"] = np.ascontiguousarray(w["L0_ffn_ln_g"][None, :])
    out["ln_b_row"] = np.ascontiguousarray(w["L0_ffn_ln_b"][None, :])
    out["bn_mean_row"] = np.ascontiguousarray(w["L0_ffn_bn_mean"][None, :])
    out["bn_var_row"] = np.ascontiguousarray(w["L0_ffn_bn_var"][None, :])
    out["bn_g_row"] = np.ascontiguousarray(w["L0_ffn_bn_g"][None, :])
    out["bn_b_row"] = np.ascontiguousarray(w["L0_ffn_bn_b"][None, :])
    out["wr"] = np.ascontiguousarray(
        np.asarray(w["L0_ffn_Wr"], np.float32).reshape(8, 128, 8)
        .transpose(1, 0, 2).reshape(128, 64))
    out["br_row"] = np.ascontiguousarray(w["L0_ffn_br"][None, :])
    out["wdown"] = np.ascontiguousarray(
        np.asarray(w["L0_ffn_W_down"], np.float32)
        .reshape(8, 8, 128, 64).transpose(2, 1, 0, 3).reshape(128, 4096))
    out["bdown8"] = np.ascontiguousarray(w["L0_ffn_b_down"])
    out["wup"] = np.ascontiguousarray(
        np.asarray(w["L0_ffn_W_up"], np.float32)
        .transpose(1, 0, 2).reshape(64, 8192))
    out["bup8"] = np.ascontiguousarray(w["L0_ffn_b_up"])
    return out


IN_SPECS = {
    "wv": ([1024, 1024], F32), "wout": ([1024, 1024], F32),
    "wsc": ([1024, 65], F32), "bsc_row": ([1, 65], F32),
    "omegaT": ([64, 32], F32), "wcT": ([128, 192], F32),
    "rff4": ([128, 128], F32), "b_v_T": ([128, 8], F32),
    "b_out_row": ([1, 1024], F32), "log_temp": ([1, 1], F32),
    "gT": ([128, 8], F32), "lnbT": ([128, 8], F32),
    "ln_g_row": ([1, 1024], F32), "ln_b_row": ([1, 1024], F32),
    "bn_mean_row": ([1, 1024], F32), "bn_var_row": ([1, 1024], F32),
    "bn_g_row": ([1, 1024], F32), "bn_b_row": ([1, 1024], F32),
    "wr": ([128, 64], F32), "br_row": ([1, 8], F32),
    "wdown": ([128, 4096], F32), "bdown8": ([8, 64], F32),
    "wup": ([64, 8192], F32), "bup8": ([8, 1024], F32),
    "TWT": ([128, 640], F32), "wpri": ([1, 8], F32),
    "ones_rowf": ([1, 128], F32), "ones_colf": ([128, 1], F32),
    "identf": ([128, 128], F32),
    "maskT": ([128, 128], BF16), "E": ([128, 256], BF16),
    "ident": ([128, 128], BF16), "allones": ([128, 128], BF16),
    "ones_row": ([1, 128], BF16), "ones_col": ([128, 1], BF16),
}


class _Rec:

    def __init__(self, eng):
        object.__setattr__(self, "_e", eng)
        object.__setattr__(self, "last", None)

    def __getattr__(self, n):
        f = getattr(object.__getattribute__(self, "_e"), n)
        if callable(f):
            def wrap(*a, **k):
                r = f(*a, **k)
                if r is not None and hasattr(r, "then_inc"):
                    object.__setattr__(self, "last", r)
                return r
            return wrap
        return f

    def inc(self, sem, n=1):
        object.__getattribute__(self, "last").then_inc(sem, n)


def build_nc(cfg):
    NQB, NKB, NV, QLAG = cfg["NQB"], cfg["NKB"], cfg["NV"], cfg["QLAG"]
    NITER, SEQ = cfg["NITER"], cfg["SEQ"]
    VLAG = cfg["VLAG"]
    NGRP = (NITER - VLAG + 3) // 4

    nc = bass.Bass("TRN2", target_bir_lowering=False,
                   detect_race_conditions=False)
    x_k = nc.dram_tensor("x_k", [NKB * 128, 1024], F32,
                         kind="ExternalInput").ap()
    y_out = nc.dram_tensor("y_out", [NQB * 128, 1024], F32,
                           kind="ExternalOutput").ap()
    zhat_dram = nc.dram_tensor("zhat_scr", [NQB * 128, 1024], BF16,
                               kind="Internal").ap()
    rb_dram = nc.dram_tensor("rb_dram", [1, 1024], F32, kind="Internal").ap()
    gin = {n: nc.dram_tensor(n, s, d, kind="ExternalInput").ap()
           for n, (s, d) in IN_SPECS.items()}
    dbg_outs = {}

    es = contextlib.ExitStack()
    sbt = lambda n, s, d: es.enter_context(nc.sbuf_tensor(n, s, d))
    pstt = lambda n, s: es.enter_context(nc.psum_tensor(n, s, F32))
    semt = lambda n: es.enter_context(nc.semaphore(n))

    with es:
        xbuf = sbt("xbuf", [128, 3, 1024], F32)
        xqbuf = sbt("xqbuf", [128, 2, 1024], F32)
        xinb = sbt("xinb", [128, 2, 1024], BF16)
        xinT = sbt("xinT", [128, 4, 8, 128], BF16)
        vwin = sbt("vwin", [128, NV, 1024], BF16)
        phiT = sbt("phiT", [64, NKB * 128], BF16)
        S_sb = sbt("S_sb", [72, 1024], BF16)
        zsum4 = sbt("zsum4", [4, 1024], F32)
        oa_bf = sbt("oa_bf", [128, 2, 1024], BF16)
        oaT = sbt("oaT", [128, 2, 8, 128], BF16)
        PTt = sbt("PTt", [128, 640], BF16)
        scf = sbt("scf", [128, 640], F32)
        t2buf = sbt("t2buf", [128, 1024], F32)
        y1buf = sbt("y1buf", [128, 1024], F32)
        zhat_sb = sbt("zhat_sb", [128, 2, 1024], BF16)
        stats1 = sbt("stats1", [128, NQB, 2], F32)
        y2buf = sbt("y2buf", [128, 2, 1024], F32)
        stage = y2buf[:, 0, :]
        stage2 = sbt("stage2", [128, 512], F32)
        wv_b = sbt("wv_b", [128, 8, 1024], BF16)
        wout_b = sbt("wout_b", [128, 8, 1024], BF16)
        wsc_b = sbt("wsc_b", [128, 8, 65], BF16)
        bscr_b = sbt("bscr_b", [1, 65], BF16)
        omegaT_b = sbt("omegaT_b", [64, 32], BF16)
        wcT_b = sbt("wcT_b", [128, 3, 64], BF16)
        wr_b = sbt("wr_b", [128, 8, 8], BF16)
        wdown_b = sbt("wdown_b", [128, 8, 8, 64], BF16)
        wup_b = sbt("wup_b", [64, 8, 1024], BF16)
        bup8_b = sbt("bup8_b", [8, 1024], BF16)
        bdown8_b = sbt("bdown8_b", [8, 64], BF16)
        lnbT_b = sbt("lnbT_b", [128, 8], BF16)
        gTc = sbt("gTc", [128, 8], F32)
        bvc_b = sbt("bvc_b", [128, 8], BF16)
        weff = sbt("weff", [128, 2, 8, 64], BF16)
        wdg_b = sbt("wdg_b", [128, 8, 64], BF16)
        wupe = sbt("wupe", [64, 2, 1024], BF16)
        bup_eb = sbt("bup_eb", [1, 1024], BF16)
        bdcol = sbt("bdcol", [64, 1], F32)
        cvec_b = sbt("cvec_b", [1, 1024], BF16)
        cTWT = sbt("cTWT", [128, 640], F32)
        cmaskT = sbt("cmaskT", [128, 128], BF16)
        cE = sbt("cE", [128, 256], BF16)
        cidentf = sbt("cidentf", [128, 128], F32)
        callones = sbt("callones", [128, 128], BF16)
        cones_row = sbt("cones_row", [1, 128], BF16)
        cones_col = sbt("cones_col", [128, 1], BF16)
        cones_rowf = sbt("cones_rowf", [1, 128], F32)
        cones_colf = sbt("cones_colf", [128, 1], F32)
        cwpri = sbt("cwpri", [1, 8], F32)
        crtemp = sbt("crtemp", [128, 1], F32)
        crff = sbt("crff", [128, 128], F32)
        ltmp = sbt("ltmp", [1, 4], F32)
        mvk = sbt("mvk", [128, 2, 16], F32)
        muk = sbt("muk", [128, 2, 1], F32)
        rvk = sbt("rvk", [128, 2, 1], F32)
        rstdk = sbt("rstdk", [128, 2, 1], F32)
        nmrk = sbt("nmrk", [128, 2, 1], F32)
        n2cb = sbt("n2cb", [128, 2, 1], F32)
        e2q = sbt("e2q", [128, 2, 1], F32)
        qvv = sbt("qvv", [128, 2, 1], F32)
        pcs = sbt("pcs", [128, 2, 1], F32)
        rnorm = sbt("rnorm", [128, 2, 1], F32)
        pc4 = sbt("pc4", [128, 4, 5], F32)
        shellU = sbt("shellU", [128, 2, 64], F32)
        shellUT = sbt("shellUT", [64, 2, 128], BF16)
        proj4s = sbt("proj4s", [128, 4, 32], F32)
        rrX = sbt("rrX", [128, 128], F32)
        rrY = sbt("rrY", [128, 128], F32)
        rrH = sbt("rrH", [128, 128], F32)
        rrT = sbt("rrT", [128, 128], F32)
        phis4 = sbt("phis4", [128, 4, 64], F32)
        tpt = sbt("tpt", [128, 2, 320], F32)
        tpTs = sbt("tpTs", [128, 3, 128], BF16)
        phtk = sbt("phtk", [128, 64], F32)
        zqb = sbt("zqb", [128, 2, 1], F32)
        rZb = sbt("rZb", [128, 2, 1], F32)
        mva = sbt("mva", [128, 2, 16], F32)
        mua = sbt("mua", [128, 2, 1], F32)
        rva = sbt("rva", [128, 2, 1], F32)
        rstda = sbt("rstda", [128, 2, 1], F32)
        nmra = sbt("nmra", [128, 2, 1], F32)
        mv1 = sbt("mv1", [128, 2, 16], F32)
        mu1 = sbt("mu1", [128, 2, 1], F32)
        rv1 = sbt("rv1", [128, 2, 1], F32)
        r1cb = sbt("r1cb", [128, 2, 1], F32)
        sg1 = sbt("sg1", [128, 2, 1], F32)
        nmr1 = sbt("nmr1", [128, 2, 1], F32)
        zhT2 = sbt("zhT2", [128, 2, 8, 128], BF16)
        hTt = sbt("hTt", [64, 2, 128], BF16)
        tb2 = sbt("tb2", [128, 1024], F32)
        sqs = y2buf[:, 1, 0:64]
        mvy = sbt("mvy", [128, 2, 16], F32)
        muy = sbt("muy", [128, 2, 1], F32)
        rvy = sbt("rvy", [128, 2, 1], F32)
        rYc = sbt("rYc", [128, 2, 1], F32)
        nmry = sbt("nmry", [128, 2, 1], F32)
        zbnT = sbt("zbnT", [128, 8], BF16)
        ohb = sbt("ohb", [128, 8], F32)
        gcol = sbt("gcol", [128, 1], F32)
        ohc = sbt("ohc", [8, 1], BF16)
        r8a = sbt("r8a", [1, 8], F32)
        r8b = sbt("r8b", [1, 8], F32)
        r8c = sbt("r8c", [1, 8], F32)
        r1a = sbt("r1a", [1, 1], F32)
        r1b = sbt("r1b", [1, 1], F32)

        ps_x = pstt("ps_x", [128, 1024])
        ps_y = pstt("ps_y", [128, 1024])
        ps_s = pstt("ps_s", [100, 1024])
        ps_a = pstt("ps_a", [128, 512])
        ps_e = pstt("ps_e", [128, 512])

        dbg_list = []
        if cfg.get("DEBUG"):
            cand = [
                ("crtemp", crtemp[:, :]), ("cvec", cvec_b[:, :]),
                ("phiT", phiT[:, 0:1024]), ("S_sb", S_sb[:, :]),
                ("vwinall", vwin[:, :, :].rearrange("p a b -> p (a b)")), ("PTt", PTt[:, :]),
                ("scf", scf[:, :]), ("stats1", stats1[:, :, :]
                 .rearrange("p a b -> p (a b)")),
                ("zhat", zhat_sb[:, 0, :]), ("t2", t2buf[:, :]),
                ("oa", oa_bf[:, 0, :]), ("zsum4", zsum4[:, :]),
                ("ohb", ohb[:, :]), ("gcol", gcol[:, :]),
                ("wdg", wdg_b[:, :, :].rearrange("p a b -> p (a b)")),
                ("bup_eb", bup_eb[:, :]), ("bdcol", bdcol[:, :]),
                ("wvb0", wv_b[:, 0, :]), ("xinT0", xinT[:, 0, :, :]
                 .rearrange("p a b -> p (a b)")),
                ("rZb", rZb[:, :, :].rearrange("p a b -> p (a b)")),
                ("y2b", y2buf[:, 0, :]), ("tb2", tb2[:, :]),
                ("hT", hTt[:, :, :].rearrange("p a b -> p (a b)")),
                ("zbnT", zbnT[:, :]),
                ("n2cb", n2cb[:, :, :].rearrange("p a b -> p (a b)")),
                ("rnorm", rnorm[:, :, :].rearrange("p a b -> p (a b)")),
                ("shellU", shellU[:, 0, :]),
                ("shellUT", shellUT[:, 0, :]),
                ("proj4s", proj4s[:, :, :].rearrange("p a b -> p (a b)")),
                ("phis4", phis4[:, :, :].rearrange("p a b -> p (a b)")),
                ("tpt", tpt[:, 0, :]),
                ("e2q", e2q[:, :, :].rearrange("p a b -> p (a b)")),
                ("qvv", qvv[:, :, :].rearrange("p a b -> p (a b)")),
                ("muk", muk[:, :, :].rearrange("p a b -> p (a b)")),
                ("rvk", rvk[:, :, :].rearrange("p a b -> p (a b)")),
                ("sqs", sqs[:, :]),
                ("tpTs", tpTs[:, :, :].rearrange("p a b -> p (a b)")),
                ("xinb", xinb[:, 0, :]),
                ("pc4", pc4[:, :, :].rearrange("p a b -> p (a b)")),
            ]
            for nm, ap in cand:
                sh = list(ap.shape)
                dbg_outs[nm] = nc.dram_tensor(
                    "dbg_" + nm, sh, ap.dtype, kind="ExternalOutput").ap()
                dbg_list.append((nm, ap))

        sem_names = [
            "s_setd", "s_setc", "s_x", "s_xq", "s_xt", "s_oat", "s_zsc",
            "s_zld", "s_zt", "s_out", "s_ln", "s_vmm", "s_vev", "s_sqr",
            "s_st0", "s_n2c", "s_phi", "s_sc", "s_pst", "s_exp", "s_msk",
            "s_pv", "s_zf", "s_pvev", "s_wmm", "s_ast", "s_wev", "s_y1",
            "s_st1", "s_zh", "s_zhs", "s_smm", "s_sev", "s_shx", "s_shT",
            "s_shev", "s_prj", "s_pjv", "s_tp", "s_tpe", "s_tpev", "s_phm",
            "s_ptk", "s_ptT", "s_r", "s_wl", "s_rb", "s_h", "s_hev", "s_ymm",
            "s_tg", "s_yst", "s_t3", "s_y2", "s_dve", "s_acd", "s_t2e",
        ]
        S = {n: semt(n) for n in sem_names}

        def kv(it):
            return it if it < NKB else None

        def k2v(it):
            k2 = it - VLAG
            return k2 if 0 <= k2 < NKB else None

        def bvf(it):
            b = it - QLAG
            return b if 0 <= b < NQB else None

        def bwf(it):
            b = it - QLAG - 1
            return b if 0 <= b < NQB else None

        def bzf(it):
            b = it - QLAG - 2
            return b if 0 <= b < NQB else None

        def cnt_b(it):
            return max(0, min(it - QLAG, NQB))

        def g2f(it):
            return (it - VLAG) // 4

        def grp_close(it):
            t = it - VLAG
            return t >= 0 and (t % 4 == 3 or it == NITER - 1)

        SETUP = []

        def sdma(engine, fn):
            SETUP.append((engine, "dma", fn))

        def scomp(engine, fn):
            SETUP.append((engine, "comp", fn))

        for nm, dst in [
            ("rff4", crff[:, :]), ("log_temp", ltmp[0:1, 0:1]),
            ("TWT", cTWT[:, :]), ("wpri", cwpri[:, :]),
            ("ones_rowf", cones_rowf[:, :]), ("ones_colf", cones_colf[:, :]),
            ("identf", cidentf[:, :]),
            ("maskT", cmaskT[:, :]), ("E", cE[:, :]),
            ("allones", callones[:, :]),
            ("ones_row", cones_row[:, :]), ("ones_col", cones_col[:, :]),
            ("gT", gTc[:, :]),
        ]:
            sdma("sync", lambda e, nm=nm, dst=dst:
                 e.dma_start(out=dst, in_=gin[nm][:, :]))
        smalls = [
            ("bsc_row", (0, 1, 0, 65), bscr_b[:, :]),
            ("omegaT", (0, 64, 96, 128), omegaT_b[:, :]),
            ("wcT", (0, 128, 128, 320),
             wcT_b[:, :, :].rearrange("p a b -> p (a b)")),
            ("wr", (0, 128, 320, 384),
             wr_b[:, :, :].rearrange("p a b -> p (a b)")),
            ("lnbT", (0, 128, 384, 392), lnbT_b[:, :]),
            ("bdown8", (0, 8, 392, 456), bdown8_b[:, :]),
            ("b_v_T", (0, 128, 456, 464), bvc_b[:, :]),
        ]
        for nm, (p0, p1, c0, c1), dst in smalls:
            sdma("sync", lambda e, nm=nm, p0=p0, p1=p1, c0=c0, c1=c1:
                 e.dma_start(out=stage2[p0:p1, c0:c1], in_=gin[nm][:, :]))
        for nm, (p0, p1, c0, c1), dst in smalls:
            scomp("scalar", lambda e, p0=p0, p1=p1, c0=c0, c1=c1, dst=dst:
                  e.activation(dst, stage2[p0:p1, c0:c1], AF.Copy))
        sdma("sync", lambda e: e.dma_start(out=stage[0:8, :],
                                           in_=gin["bup8"][:, :]))
        scomp("scalar", lambda e: e.activation(bup8_b[:, :], stage[0:8, :],
                                               AF.Copy))
        big = []
        for i in range(8):
            big.append((gin["wv"][128 * i:128 * (i + 1), :], stage[:, :],
                        wv_b[:, i, :]))
        for i in range(8):
            big.append((gin["wout"][128 * i:128 * (i + 1), :], stage[:, :],
                        wout_b[:, i, :]))
        big.append((gin["wsc"][:, :].rearrange("(c p) o -> p c o", p=128),
                    stage[:, 0:520].rearrange("p (c o) -> p c o", o=65),
                    wsc_b[:, :, :].rearrange("p a b -> p (a b)")))
        for i in range(4):
            big.append((gin["wdown"][:, 1024 * i:1024 * (i + 1)], stage[:, :],
                        wdown_b[:, 2 * i:2 * i + 2, :, :]
                        .rearrange("p a e o -> p (a e o)")))
        for i in range(8):
            big.append((gin["wup"][:, 1024 * i:1024 * (i + 1)],
                        stage[0:64, :], wup_b[0:64, i, :]))
        for src, st_ap, dst in big:
            sdma("sync", lambda e, src=src, st_ap=st_ap:
                 e.dma_start(out=st_ap, in_=src))
            flat = st_ap if len(st_ap.shape) == 2 else st_ap.rearrange(
                "p a b -> p (a b)")
            scomp("scalar", lambda e, st_ap=flat, dst=dst:
                  e.activation(dst, st_ap, AF.Copy))
        sdma("sync", lambda e: e.dma_start(out=stage[0:1, 0:1024],
                                           in_=gin["b_out_row"][:, :]))
        scomp("scalar", lambda e: e.activation(ltmp[0:1, 1:2],
                                               ltmp[0:1, 0:1], AF.Exp))
        scomp("vector", lambda e: e.tensor_scalar(ltmp[0:1, 1:2],
                                                  ltmp[0:1, 1:2], 1.0, None,
                                                  ALU.add))
        scomp("scalar", lambda e: e.activation(ltmp[0:1, 1:2],
                                               ltmp[0:1, 1:2], AF.Ln))
        scomp("vector", lambda e: e.tensor_scalar(ltmp[0:1, 1:2],
                                                  ltmp[0:1, 1:2], 0.1, None,
                                                  ALU.add))
        scomp("vector", lambda e: e.reciprocal(ltmp[0:1, 2:3],
                                               ltmp[0:1, 1:2]))
        scomp("tensor", lambda e: e.matmul(ps_e[:, 0:1], cones_rowf[:, :],
                                           ltmp[0:1, 2:3], start=True,
                                           stop=True))
        scomp("vector", lambda e: e.tensor_copy(crtemp[:, :], ps_e[:, 0:1]))

        def _cvec_mms(e):
            for dc in range(8):
                e.matmul(ps_y[0:1, 0:512], bvc_b[:, dc:dc + 1],
                         wout_b[:, dc, 0:512], start=(dc == 0),
                         stop=(dc == 7))
                e.matmul(ps_y[0:1, 512:1024], bvc_b[:, dc:dc + 1],
                         wout_b[:, dc, 512:1024], start=(dc == 0),
                         stop=(dc == 7))
        scomp("tensor", _cvec_mms)
        scomp("vector", lambda e: e.tensor_add(stage[0:1, 0:1024],
                                               ps_y[0:1, 0:1024],
                                               stage[0:1, 0:1024]))
        scomp("vector", lambda e: e.tensor_copy(cvec_b[:, :],
                                                stage[0:1, 0:1024]))
        scomp("vector", lambda e: e.memset(S_sb[:, :], 0.0))
        scomp("vector", lambda e: e.memset(zsum4[:, :], 0.0))
        scomp("vector", lambda e: e.memset(phiT[:, :], 0.0))

        SET_D = 16 * sum(1 for _, kk, _f in SETUP if kk == "dma")
        SET_C = sum(1 for _, kk, _f in SETUP if kk == "comp")

        def emit_setup(eng, engine_name):
            d = c = 0
            for engn, kind, fn in SETUP:
                if engn == engine_name:
                    if d:
                        eng.wait_ge(S["s_setd"], 16 * d)
                    if c:
                        eng.wait_ge(S["s_setc"], c)
                    r = fn(eng)
                    if r is None:
                        r = eng.last if eng.last is not None else eng.nop()
                    if kind == "dma":
                        r.then_inc(S["s_setd"], 16)
                    else:
                        r.then_inc(S["s_setc"], 1)
                if kind == "dma":
                    d += 1
                else:
                    c += 1

        def setup_barrier(eng):
            eng.wait_ge(S["s_setd"], SET_D)
            eng.wait_ge(S["s_setc"], SET_C)

        grp_last = {}
        for it in range(VLAG, NITER):
            g = g2f(it)
            if bzf(it) is not None:
                grp_last[g] = ("z", it)
            if k2v(it) is not None:
                grp_last[g] = ("s", it)
        smm_cnt = {}
        zhs_cnt = {}
        cs = cz = 0
        for it in range(NITER):
            if k2v(it) is not None:
                cs += 1
            if bzf(it) is not None:
                cz += 1
            smm_cnt[it] = cs
            zhs_cnt[it] = cz

        rctr = [0]
        wlc = [7]

        with nc.Block() as block:

            @block.sync
            def _(eng):
                eng = _Rec(eng)
                emit_setup(eng, "sync")
                setup_barrier(eng)
                for it in range(NITER):
                    k, b = kv(it), bvf(it)
                    if k is not None:
                        if k >= 3:
                            eng.wait_ge(S["s_ln"], k - 2)
                        eng.wait_ge(S["s_x"], 16 * k)
                        eng.dma_start(out=xbuf[:, k % 3, :],
                                      in_=x_k[128 * k:128 * (k + 1), :]
                                      ).then_inc(S["s_x"], 16)
                    if b is not None:
                        if b >= 2:
                            eng.wait_ge(S["s_y1"], b - 1)
                        eng.wait_ge(S["s_xq"], 16 * b)
                        eng.dma_start(
                            out=xqbuf[:, b % 2, :],
                            in_=x_k[512 + 128 * b:512 + 128 * (b + 1), :]
                        ).then_inc(S["s_xq"], 16)
                eng.wait_ge(S["s_sev"], NGRP)
                for wi, (nm, dst) in enumerate(
                        [("ln_g_row", xbuf[0:1, 0, 0:1024]),
                         ("ln_b_row", xbuf[0:1, 1, 0:1024]),
                         ("bn_mean_row", xbuf[0:1, 2, 0:1024]),
                         ("bn_var_row", y1buf[0:1, 0:1024]),
                         ("bn_g_row", xqbuf[0:1, 0, 0:1024]),
                         ("bn_b_row", xqbuf[0:1, 1, 0:1024]),
                         ("br_row", stage2[0:1, 0:8])]):
                    eng.wait_ge(S["s_wl"], 16 * wi)
                    eng.dma_start(out=dst, in_=gin[nm][:, :]
                                  ).then_inc(S["s_wl"], 16)
                eng.wait_ge(S["s_zsc"], 16 * NQB)
                for j in range(NQB):
                    if j >= 2:
                        eng.wait_ge(S["s_y2"], j - 1)
                    eng.wait_ge(S["s_zld"], 16 * j)
                    eng.dma_start(out=zhat_sb[:, j % 2, :],
                                  in_=zhat_dram[128 * j:128 * (j + 1), :]
                                  ).then_inc(S["s_zld"], 16)
                    eng.wait_ge(S["s_zld"], 16 * (j + 1))
                    eng.wait_ge(S["s_zt"], 16 * j)
                    if j >= 2:
                        eng.wait_ge(S["s_h"], j - 1)
                    eng.dma_start(out=zhT2[:, j % 2, :, :],
                                  in_=zhat_sb[:, j % 2, :]
                                  .rearrange("p (c f) -> p c f", f=128),
                                  transpose=True).then_inc(S["s_zt"], 16)

            @block.tensor
            def _(eng):
                eng = _Rec(eng)
                emit_setup(eng, "tensor")
                setup_barrier(eng)
                started = {}
                for it in range(NITER):
                    k, b, bz = k2v(it), bvf(it), bzf(it)
                    g = g2f(it)
                    if k is not None:
                        eng.wait_ge(S["s_xt"], 16 * (k + 1))
                        eng.wait_ge(S["s_wev"], cnt_b(it))
                        if k >= 1:
                            eng.wait_ge(S["s_shx"], k)
                            eng.wait_ge(S["s_zf"], cnt_b(it))
                            eng.wait_ge(S["s_tpev"], 4 * max(g, 0))
                        p = k % 2
                        for dc in range(8):
                            lhs = xinT[:, k % 4, dc, :]
                            eng.matmul(ps_y[:, 0:512], lhs,
                                       wv_b[:, dc, 0:512],
                                       start=(dc == 0), stop=(dc == 7))
                            eng.matmul(ps_y[:, 512:1024], lhs,
                                       wv_b[:, dc, 512:1024],
                                       start=(dc == 0), stop=(dc == 7))
                            eng.matmul(ps_a[:, 0:65], lhs, wsc_b[:, dc, :],
                                       start=(dc == 0), stop=False)
                        eng.matmul(ps_a[:, 0:65], cones_row[:, :],
                                   bscr_b[:, :], start=False, stop=True)
                        eng.inc(S["s_vmm"], 1)
                    if b is not None:
                        eng.wait_ge(S["s_msk"], b + 1)
                        eng.wait_ge(S["s_vev"], b + 5)
                        eng.wait_ge(S["s_pst"], b + 1)
                        if b >= 1:
                            eng.wait_ge(S["s_sev"], (b - 1) // 4 + 1)
                        for c in range(5):
                            lhs = PTt[:, 128 * c:128 * (c + 1)]
                            vs = (b + c) % NV
                            last = (c == 4 and b == 0)
                            eng.matmul(ps_x[:, 0:512], lhs,
                                       vwin[:, vs, 0:512],
                                       start=(c == 0), stop=last)
                            eng.matmul(ps_x[:, 512:1024], lhs,
                                       vwin[:, vs, 512:1024],
                                       start=(c == 0), stop=last)
                            eng.matmul(ps_a[:, 72:73], lhs, cones_col[:, :],
                                       start=(c == 0), stop=(c == 4))
                        if b >= 1:
                            eng.matmul(ps_x[:, 0:512], callones[0:b, :],
                                       S_sb[0:b, 0:512], start=False,
                                       stop=True)
                            eng.matmul(ps_x[:, 512:1024], callones[0:b, :],
                                       S_sb[0:b, 512:1024], start=False,
                                       stop=True)
                        eng.inc(S["s_pv"], 1)
                    if (it - VLAG) % 4 == 0 and g >= 1:
                        eng.wait_ge(S["s_sev"], g)
                    if bz is not None:
                        eng.wait_ge(S["s_zh"], bz + 1)
                        row = 96 + ((it - VLAG) % 4)
                        st = g not in started
                        started[g] = True
                        lhs = cE[:, 155 - row:155 - row + 100]
                        last = grp_last[g] == ("z", it)
                        eng.matmul(ps_s[:, 0:512], lhs,
                                   zhat_sb[:, bz % 2, 0:512],
                                   start=st, stop=last)
                        eng.matmul(ps_s[:, 512:1024], lhs,
                                   zhat_sb[:, bz % 2, 512:1024],
                                   start=st, stop=last)
                        eng.inc(S["s_zhs"], 1)
                    if k is not None:
                        eng.wait_ge(S["s_vev"], k + 1)
                        st = g not in started
                        started[g] = True
                        lhs = cE[:, 155 - k:155 - k + 100]
                        last = grp_last[g] == ("s", it)
                        eng.matmul(ps_s[:, 0:512], lhs,
                                   vwin[:, k % NV, 0:512], start=st,
                                   stop=last)
                        eng.matmul(ps_s[:, 512:1024], lhs,
                                   vwin[:, k % NV, 512:1024], start=st,
                                   stop=last)
                        eng.inc(S["s_smm"], 1)
                    if k is not None:
                        eng.wait_ge(S["s_shx"], k + 1)
                        if k >= 1:
                            eng.wait_ge(S["s_shev"], k)
                        eng.transpose(ps_e[0:64, 128:256],
                                      shellU[:, k % 2, :], cidentf[:, :])
                        eng.inc(S["s_shT"], 1)
                        eng.wait_ge(S["s_shev"], k + 1)
                        if k >= 1:
                            eng.wait_ge(S["s_pjv"], k)
                        eng.matmul(ps_e[:, 0:32], shellUT[:, k % 2, :],
                                   omegaT_b[:, :], start=True, stop=True)
                        eng.inc(S["s_prj"], 1)
                    if b is not None:
                        eng.wait_ge(S["s_oat"], 16 * (b + 1))
                        eng.wait_ge(S["s_vev"], min(it - VLAG + 1, NKB))
                        for dc in range(8):
                            lhs = oaT[:, b % 2, dc, :]
                            eng.matmul(ps_y[:, 0:512], lhs,
                                       wout_b[:, dc, 0:512],
                                       start=(dc == 0), stop=False)
                            eng.matmul(ps_y[:, 512:1024], lhs,
                                       wout_b[:, dc, 512:1024],
                                       start=(dc == 0), stop=False)
                        eng.matmul(ps_y[:, 0:512], cones_row[:, :],
                                   cvec_b[0:1, 0:512], start=False,
                                   stop=True)
                        eng.matmul(ps_y[:, 512:1024], cones_row[:, :],
                                   cvec_b[0:1, 512:1024], start=False,
                                   stop=True)
                        eng.inc(S["s_wmm"], 1)
                    if grp_close(it):
                        ks = [kk for kk in range(4 * g, 4 * g + 4)
                              if kk < NKB]
                        for kb in ks:
                            eng.wait_ge(S["s_tp"], kb + 1)
                            if kb >= 1:
                                eng.wait_ge(S["s_tpev"], kb)
                            par = kb % 2
                            eng.transpose(ps_a[:, 128:256],
                                          tpt[:, par, 0:128], cidentf[:, :])
                            eng.transpose(ps_a[:, 256:384],
                                          tpt[:, par, 128:256],
                                          cidentf[:, :])
                            eng.transpose(ps_a[0:64, 384:512],
                                          tpt[:, par, 256:320],
                                          cidentf[:, :])
                            eng.inc(S["s_tpe"], 1)
                            eng.wait_ge(S["s_tpev"], kb + 1)
                            if kb >= 1:
                                eng.wait_ge(S["s_ptk"], kb)
                            for c in range(3):
                                npp = 128 if c < 2 else 64
                                eng.matmul(ps_e[:, 256:320],
                                           tpTs[0:npp, c, :],
                                           wcT_b[0:npp, c, :],
                                           start=(c == 0), stop=(c == 2))
                            eng.inc(S["s_phm"], 1)
                            eng.wait_ge(S["s_ptk"], kb + 1)
                            if kb >= 1:
                                eng.wait_ge(S["s_phi"], kb)
                            eng.transpose(ps_e[0:64, 320:448], phtk[:, :],
                                          cidentf[:, :])
                            eng.inc(S["s_ptT"], 1)
                    bs = it - QLAG + 1
                    if 0 <= bs < NQB:
                        eng.wait_ge(S["s_phi"], bs + 5)
                        eng.wait_ge(S["s_pvev"], bs)
                        qcol = 128 * (bs + 4)
                        for c in range(5):
                            kcol = 128 * (bs + c)
                            eng.matmul(ps_x[:, 128 * c:128 * (c + 1)],
                                       phiT[0:64, kcol:kcol + 128],
                                       phiT[0:64, qcol:qcol + 128],
                                       start=(c in (0, 4)),
                                       stop=(c in (3, 4)))
                        eng.inc(S["s_sc"], 1)

                eng.wait_ge(S["s_r"], 1)
                eng.matmul(ps_x[0:1, 0:512], cones_colf[0:4, :],
                           zsum4[:, 0:512], start=True, stop=True)
                eng.matmul(ps_x[0:1, 512:1024], cones_colf[0:4, :],
                           zsum4[:, 512:1024], start=True, stop=True)
                eng.inc(S["s_r"], 1)
                eng.wait_ge(S["s_rb"], 16 * 2)
                for c in range(8):
                    eng.matmul(ps_e[0:1, 384:392], zbnT[:, c:c + 1],
                               wr_b[:, c, :], start=(c == 0), stop=(c == 7))
                eng.inc(S["s_r"], 1)
                eng.wait_ge(S["s_r"], 9)
                eng.matmul(ps_e[:, 448:456], cones_rowf[:, :], r8b[:, :],
                           start=True, stop=True)
                eng.matmul(ps_e[:, 456:457], cones_rowf[:, :], r1a[:, :],
                           start=True, stop=True)
                eng.inc(S["s_r"], 1)
                eng.wait_ge(S["s_r"], 11)
                eng.wait_ge(S["s_rb"], 16 * 4)
                eng.matmul(ps_x[0:1, 0:512], ohc[:, :], bup8_b[:, 0:512],
                           start=True, stop=True)
                eng.matmul(ps_x[0:1, 512:1024], ohc[:, :],
                           bup8_b[:, 512:1024], start=True, stop=True)
                eng.matmul(ps_e[0:1, 384:448], ohc[:, :], bdown8_b[:, :],
                           start=True, stop=False)
                for dc in range(8):
                    eng.matmul(ps_e[0:1, 384:448], lnbT_b[:, dc:dc + 1],
                               weff[:, 1, dc, :], start=False,
                               stop=(dc == 7))
                eng.inc(S["s_r"], 1)

                eng.wait_ge(S["s_r"], 14)
                eng.wait_ge(S["s_rb"], 16 * 6)
                for j in range(NQB):
                    eng.wait_ge(S["s_zt"], 16 * (j + 1))
                    if j >= 1:
                        eng.wait_ge(S["s_hev"], j)
                    for dc in range(8):
                        eng.matmul(ps_a[0:64, 0:128], wdg_b[:, dc, :],
                                   zhT2[:, j % 2, dc, :],
                                   start=(dc == 0), stop=(dc == 7))
                    eng.inc(S["s_h"], 1)
                    eng.wait_ge(S["s_hev"], j + 1)
                    if j >= 1:
                        eng.wait_ge(S["s_tg"], j)
                    eng.matmul(ps_x[:, 0:512], hTt[:, j % 2, :],
                               wupe[0:64, 0, 0:512], start=True, stop=False)
                    eng.matmul(ps_x[:, 512:1024], hTt[:, j % 2, :],
                               wupe[0:64, 0, 512:1024], start=True,
                               stop=False)
                    eng.matmul(ps_x[:, 0:512], cones_row[:, :],
                               bup_eb[0:1, 0:512], start=False, stop=True)
                    eng.matmul(ps_x[:, 512:1024], cones_row[:, :],
                               bup_eb[0:1, 512:1024], start=False,
                               stop=True)
                    eng.inc(S["s_ymm"], 1)

            @block.vector
            def _(eng):
                eng = _Rec(eng)
                emit_setup(eng, "vector")
                setup_barrier(eng)

                dvec = [0]

                def dsync():
                    eng.inc(S["s_dve"], 1)
                    dvec[0] += 1
                    eng.wait_ge(S["s_dve"], dvec[0])

                def ln_stats(src_ap, mv, mu_o, rv_o, p):
                    eng.bn_stats(mv[:, p, 0:6], src_ap[:, 0:512])
                    eng.bn_stats(mv[:, p, 6:12], src_ap[:, 512:1024])
                    eng.bn_aggr(mv[:, p, 12:14], mv[:, p, 0:6])
                    eng.bn_aggr(mv[:, p, 14:16], mv[:, p, 6:12])
                    dsync()
                    eng.tensor_scalar(mu_o[:, p, :], mv[:, p, 12:13], 0.5,
                                      None, ALU.mult)
                    eng.tensor_tensor(mv[:, p, 0:1], mv[:, p, 12:13],
                                      mv[:, p, 14:15], ALU.subtract)
                    eng.tensor_tensor(rv_o[:, p, :], mv[:, p, 13:14],
                                      mv[:, p, 15:16], ALU.add)
                    dsync()
                    eng.scalar_tensor_tensor(mu_o[:, p, :], mv[:, p, 14:15],
                                             0.5, mu_o[:, p, :], ALU.mult,
                                             ALU.add)
                    eng.tensor_scalar(mv[:, p, 0:1], mv[:, p, 0:1], 0.5,
                                      None, ALU.mult)
                    eng.tensor_scalar(rv_o[:, p, :], rv_o[:, p, :], 0.5,
                                      EPS, ALU.mult, ALU.add)
                    dsync()
                    eng.tensor_tensor(mv[:, p, 1:2], mv[:, p, 0:1],
                                      mv[:, p, 0:1], ALU.mult)
                    dsync()
                    eng.tensor_tensor(rv_o[:, p, :], rv_o[:, p, :],
                                      mv[:, p, 1:2], ALU.add)

                def sinpoly(dst_ap, x_ap, y_ap, t_ap):
                    a = list(reversed(SIN_C[1:]))
                    eng.tensor_scalar(t_ap, y_ap, a[0], None, ALU.mult)
                    for cc in a[1:]:
                        eng.scalar_tensor_tensor(t_ap, t_ap, cc, y_ap,
                                                 ALU.add, ALU.mult)
                    eng.scalar_tensor_tensor(dst_ap, t_ap, SIN_C[0], x_ap,
                                             ALU.add, ALU.mult)

                for it in range(NITER):
                    kf, b, bz = kv(it), bvf(it), bzf(it)
                    k = k2v(it)
                    bw = bwf(it)
                    g = g2f(it)
                    if kf is not None:
                        eng.wait_ge(S["s_x"], 16 * (kf + 1))
                        ln_stats(xbuf[:, kf % 3, :], mvk, muk, rvk, kf % 2)
                        eng.inc(S["s_st0"], 1)
                    if b is not None:
                        eng.wait_ge(S["s_sc"], b + 1)
                        if b >= 1:
                            eng.wait_ge(S["s_exp"], b)
                        eng.scalar_tensor_tensor(scf[:, :], ps_x[:, 0:640],
                                                 crtemp[:, 0:1], cTWT[:, :],
                                                 ALU.mult, ALU.mult)
                        eng.inc(S["s_pst"], 1)
                    if k is not None:
                        p = k % 2
                        eng.wait_ge(S["s_sqr"], k + 1)
                        eng.tensor_scalar(n2cb[:, p, :], n2cb[:, p, :],
                                          1e-24, None, ALU.max)
                        eng.inc(S["s_n2c"], 1)
                        vfac = 0.0 if k < 4 else 1.0
                        eng.tensor_scalar(pcs[:, p, :], e2q[:, p, :],
                                          1.0, None, ALU.add)
                        dsync()
                        eng.reciprocal(qvv[:, p, :], pcs[:, p, :])
                        dsync()
                        eng.tensor_scalar(qvv[:, p, :], qvv[:, p, :], -2.0,
                                          1.0, ALU.mult, ALU.add)
                        sl = k % 4
                        rs = RFF_SCALE
                        eng.memset(pc4[:, sl, 0:1], MAGS[0] * rs * vfac)
                        dsync()
                        eng.tensor_scalar(pc4[:, sl, 1:2], qvv[:, p, :],
                                          MAGS[1] * rs * vfac, None,
                                          ALU.mult)
                        eng.tensor_tensor(pc4[:, sl, 2:3], qvv[:, p, :],
                                          qvv[:, p, :], ALU.mult)
                        dsync()
                        eng.tensor_scalar(pc4[:, sl, 3:4], pc4[:, sl, 2:3],
                                          MAGS[3] * rs * vfac, None,
                                          ALU.mult)
                        eng.tensor_tensor(pc4[:, sl, 4:5], pc4[:, sl, 2:3],
                                          pc4[:, sl, 2:3], ALU.mult)
                        eng.tensor_scalar(pc4[:, sl, 2:3], pc4[:, sl, 2:3],
                                          MAGS[2] * rs * vfac, None,
                                          ALU.mult)
                        dsync()
                        eng.tensor_tensor(pc4[:, sl, 3:4], pc4[:, sl, 3:4],
                                          qvv[:, p, :], ALU.mult)
                        eng.tensor_scalar(pc4[:, sl, 4:5], pc4[:, sl, 4:5],
                                          MAGS[4] * rs * vfac, None,
                                          ALU.mult)
                    if b is not None:
                        eng.wait_ge(S["s_exp"], b + 1)
                        eng.tensor_tensor(PTt[:, 512:640], PTt[:, 512:640],
                                          cmaskT[:, :], ALU.mult)
                        eng.inc(S["s_msk"], 1)
                        eng.wait_ge(S["s_pv"], b + 1)
                        p = b % 2
                        coeff = float(max(0, 128 * b - 512)
                                      - max(0, 512 - 128 * b))
                        eng.tensor_scalar(zqb[:, p, :], ps_a[:, 72:73],
                                          coeff, None, ALU.add)
                        dsync()
                        eng.reciprocal(rZb[:, p, :], zqb[:, p, :])
                        eng.inc(S["s_zf"], 1)
                    if k is not None:
                        eng.wait_ge(S["s_shT"], k + 1)
                        if k >= 1:
                            eng.wait_ge(S["s_prj"], k)
                        eng.tensor_copy(shellUT[:, k % 2, :],
                                        ps_e[0:64, 128:256])
                        eng.inc(S["s_shev"], 1)
                        eng.wait_ge(S["s_prj"], k + 1)
                        eng.tensor_copy(proj4s[:, k % 4, :], ps_e[:, 0:32])
                        eng.inc(S["s_pjv"], 1)
                    if grp_close(it):
                        ks = [kk for kk in range(4 * g, 4 * g + 4)
                              if kk < NKB]
                        if ks:
                            eng.wait_ge(S["s_pjv"], ks[-1] + 1)
                            nblk = len(ks)
                            w = 32 * nblk
                            pr3 = proj4s[:, 0:nblk, :]
                            eng.scalar_tensor_tensor(
                                rrX[:, 0:w],
                                pr3.rearrange("p a b -> p (a b)"), 1.0,
                                crff[:, 0:w], ALU.mult, ALU.add)
                            eng.tensor_scalar(rrY[:, 0:w], rrX[:, 0:w],
                                              1.0 / (2 * PI), None, ALU.mult)
                            eng.tensor_scalar(rrH[:, 0:w], rrY[:, 0:w],
                                              0.5, None, ALU.is_ge)
                            for j in range(1, 5):
                                if w < 64:
                                    dsync()
                                eng.scalar_tensor_tensor(
                                    rrH[:, 0:w], rrY[:, 0:w], j + 0.5,
                                    rrH[:, 0:w], ALU.is_ge, ALU.add)
                            eng.tensor_scalar(rrT[:, 0:w], rrY[:, 0:w],
                                              -0.5, None, ALU.is_le)
                            for j in range(1, 5):
                                if w < 64:
                                    dsync()
                                eng.scalar_tensor_tensor(
                                    rrT[:, 0:w], rrY[:, 0:w], -(j + 0.5),
                                    rrT[:, 0:w], ALU.is_le, ALU.add)
                            if w < 64:
                                dsync()
                            eng.tensor_tensor(rrH[:, 0:w], rrH[:, 0:w],
                                              rrT[:, 0:w], ALU.subtract)
                            if w < 64:
                                dsync()
                            eng.scalar_tensor_tensor(rrY[:, 0:w],
                                                     rrH[:, 0:w], -2 * PI,
                                                     rrX[:, 0:w], ALU.mult,
                                                     ALU.add)
                            eng.tensor_tensor(rrH[:, 0:w], rrY[:, 0:w],
                                              rrY[:, 0:w], ALU.mult)
                            sin_dst = phis4[:, 0:nblk, 32:64]
                            sinpoly(sin_dst, rrY[:, 0:w]
                                    .rearrange("p (a b) -> p a b", b=32),
                                    rrH[:, 0:w]
                                    .rearrange("p (a b) -> p a b", b=32),
                                    rrT[:, 0:w]
                                    .rearrange("p (a b) -> p a b", b=32))
                            eng.tensor_scalar(rrT[:, 0:w], rrY[:, 0:w],
                                              PI / 2, None, ALU.is_ge)
                            if w < 64:
                                dsync()
                            eng.scalar_tensor_tensor(rrY[:, 0:w],
                                                     rrT[:, 0:w], -2 * PI,
                                                     rrY[:, 0:w], ALU.mult,
                                                     ALU.add)
                            eng.tensor_scalar(rrY[:, 0:w], rrY[:, 0:w],
                                              PI / 2, None, ALU.add)
                            eng.tensor_tensor(rrH[:, 0:w], rrY[:, 0:w],
                                              rrY[:, 0:w], ALU.mult)
                            cos_dst = phis4[:, 0:nblk, 0:32]
                            sinpoly(cos_dst, rrY[:, 0:w]
                                    .rearrange("p (a b) -> p a b", b=32),
                                    rrH[:, 0:w]
                                    .rearrange("p (a b) -> p a b", b=32),
                                    rrT[:, 0:w]
                                    .rearrange("p (a b) -> p a b", b=32))
                            for kb in ks:
                                sl = kb % 4
                                par = kb % 2
                                if kb >= 2:
                                    eng.wait_ge(S["s_tpe"], kb - 1)
                                for m in range(5):
                                    eng.tensor_scalar(
                                        tpt[:, par, 64 * m:64 * (m + 1)],
                                        phis4[:, sl, :],
                                        pc4[:, sl, m:m + 1], None,
                                        ALU.mult)
                                eng.inc(S["s_tp"], 1)
                                eng.wait_ge(S["s_tpe"], kb + 1)
                                eng.tensor_copy(tpTs[:, 0, :],
                                                ps_a[:, 128:256])
                                eng.tensor_copy(tpTs[:, 1, :],
                                                ps_a[:, 256:384])
                                eng.tensor_copy(tpTs[0:64, 2, :],
                                                ps_a[0:64, 384:512])
                                eng.inc(S["s_tpev"], 1)
                                eng.wait_ge(S["s_phm"], kb + 1)
                                eng.tensor_copy(phtk[:, :],
                                                ps_e[:, 256:320])
                                eng.inc(S["s_ptk"], 1)
                                eng.wait_ge(S["s_ptT"], kb + 1)
                                eng.tensor_copy(
                                    phiT[0:64, 128 * kb:128 * (kb + 1)],
                                    ps_e[0:64, 320:448])
                                eng.inc(S["s_phi"], 1)
                    if bw is not None:
                        eng.wait_ge(S["s_wev"], bw + 1)
                        ln_stats(tb2, mva, mua, rva, bw % 2)
                        eng.inc(S["s_ast"], 1)
                        eng.wait_ge(S["s_t2e"], bw + 1)
                        eng.wait_ge(S["s_xq"], 16 * (bw + 1))
                        eng.tensor_add(y1buf[:, :], t2buf[:, :],
                                       xqbuf[:, bw % 2, :])
                        eng.inc(S["s_y1"], 1)
                        ln_stats(y1buf, mv1, mu1, rv1, bw % 2)
                        eng.inc(S["s_st1"], 1)
                    if grp_close(it):
                        ks = [kk for kk in range(4 * g, 4 * g + 4)
                              if kk < NKB]
                        if ks:
                            eng.wait_ge(S["s_smm"], smm_cnt[it])
                        eng.wait_ge(S["s_zhs"], zhs_cnt[it])
                        bzs = [bzf(i2) for i2 in
                               range(4 * g + VLAG,
                                     min(4 * g + 4 + VLAG, NITER))
                               if bzf(i2) is not None]
                        if ks:
                            eng.tensor_tensor(S_sb[:, :], S_sb[:, :],
                                              ps_s[0:72, :], ALU.add)
                        if bzs:
                            eng.tensor_tensor(zsum4[0:4, :], zsum4[0:4, :],
                                              ps_s[96:100, :], ALU.add)
                        eng.inc(S["s_sev"], 1)

                eng.wait_ge(S["s_sev"], NGRP)
                eng.wait_ge(S["s_wl"], 16 * 7)
                eng.inc(S["s_r"], 1)
                eng.wait_ge(S["s_r"], 2)
                eng.tensor_scalar(stage[0:1, 0:1024], ps_x[0:1, 0:1024],
                                  1.0 / SEQ, None, ALU.mult)
                eng.tensor_tensor(stage[0:1, 0:1024], stage[0:1, 0:1024],
                                  xbuf[0:1, 0, 0:1024], ALU.mult)
                eng.tensor_tensor(stage[0:1, 0:1024], stage[0:1, 0:1024],
                                  xbuf[0:1, 1, 0:1024], ALU.add)
                eng.tensor_scalar(tb2[0:1, 0:1024], y1buf[0:1, 0:1024],
                                  EPS, None, ALU.add)
                eng.inc(S["s_r"], 1)
                eng.wait_ge(S["s_r"], 4)
                eng.tensor_tensor(tb2[0:1, 0:1024], tb2[0:1, 0:1024],
                                  xqbuf[0:1, 0, 0:1024], ALU.mult)
                eng.tensor_tensor(stage[0:1, 0:1024], stage[0:1, 0:1024],
                                  xbuf[0:1, 2, 0:1024], ALU.subtract)
                eng.tensor_tensor(stage[0:1, 0:1024], stage[0:1, 0:1024],
                                  tb2[0:1, 0:1024], ALU.mult)
                eng.tensor_tensor(stage[0:1, 0:1024], stage[0:1, 0:1024],
                                  xqbuf[0:1, 1, 0:1024], ALU.add)
                eng.inc(S["s_r"], 1)
                eng.wait_ge(S["s_r"], 6)
                eng.tensor_add(r8a[:, :], ps_e[0:1, 384:392],
                               stage2[0:1, 0:8])
                dsync()
                eng.tensor_reduce(r1a[:, :], r8a[:, :],
                                  mybir.AxisListType.X, ALU.max)
                dsync()
                eng.tensor_scalar(r1b[:, :], r1a[:, :], -1.0, None,
                                  ALU.mult)
                eng.inc(S["s_r"], 1)
                eng.wait_ge(S["s_r"], 8)
                eng.tensor_reduce(r1a[:, :], r8a[:, :],
                                  mybir.AxisListType.X, ALU.add)
                dsync()
                eng.reciprocal(r1b[:, :], r1a[:, :])
                dsync()
                eng.tensor_tensor(r8b[:, :], r8a[:, :],
                                  r1b[0:1, 0:1].to_broadcast([1, 8]),
                                  ALU.mult)
                dsync()
                eng.tensor_reduce(r1a[:, :], r8b[:, :],
                                  mybir.AxisListType.X, ALU.max)
                dsync()
                eng.tensor_scalar(r8c[:, :], r8b[:, :], r1a[0:1, 0:1],
                                  None, ALU.is_ge)
                dsync()
                eng.tensor_tensor(r8c[:, :], r8c[:, :], cwpri[:, :],
                                  ALU.mult)
                dsync()
                eng.tensor_reduce(r1b[:, :], r8c[:, :],
                                  mybir.AxisListType.X, ALU.max)
                dsync()
                eng.tensor_scalar(r8b[:, :], r8c[:, :], r1b[0:1, 0:1],
                                  None, ALU.is_ge)
                dsync()
                eng.tensor_copy(stage2[0:1, 16:24], r8b[:, :])
                eng.inc(S["s_r"], 1)
                eng.wait_ge(S["s_r"], 10)
                eng.tensor_copy(ohb[:, :], ps_e[:, 448:456])
                eng.tensor_copy(gcol[:, :], ps_e[:, 456:457])
                for e8 in range(8):
                    src = wdown_b[:, :, e8, :]
                    if e8 == 0:
                        eng.tensor_scalar(weff[:, 0, :, :], src,
                                          ohb[:, 0:1], None, ALU.mult)
                    else:
                        eng.scalar_tensor_tensor(
                            weff[:, e8 % 2, :, :], src, ohb[:, e8:e8 + 1],
                            weff[:, 1 - (e8 % 2), :, :], ALU.mult, ALU.add)
                for dc in range(8):
                    eng.tensor_scalar(wdg_b[:, dc, :], weff[:, 1, dc, :],
                                      gTc[:, dc:dc + 1], None, ALU.mult)
                for e8 in range(8):
                    src = wup_b[0:64, e8, :]
                    if e8 == 0:
                        eng.tensor_scalar(wupe[0:64, 0, :], src,
                                          ohb[0:64, 0:1], None, ALU.mult)
                    else:
                        eng.scalar_tensor_tensor(
                            wupe[0:64, e8 % 2, :], src,
                            ohb[0:64, e8:e8 + 1],
                            wupe[0:64, 1 - (e8 % 2), :], ALU.mult, ALU.add)
                eng.tensor_copy(wupe[0:64, 0, :], wupe[0:64, 1, :])
                eng.inc(S["s_r"], 1)
                eng.wait_ge(S["s_r"], 12)
                eng.tensor_copy(bup_eb[:, :], ps_x[0:1, 0:1024])
                eng.tensor_copy(stage2[0:1, 32:96], ps_e[0:1, 384:448])
                eng.inc(S["s_r"], 1)
                eng.nop()
                eng.inc(S["s_r"], 1)

                for j in range(NQB):
                    eng.wait_ge(S["s_tg"], j + 1)
                    ln_stats(tb2, mvy, muy, rvy, j % 2)
                    eng.inc(S["s_yst"], 1)
                    eng.wait_ge(S["s_zld"], 16 * (j + 1))
                    eng.tensor_scalar(y1buf[:, :], zhat_sb[:, j % 2, :],
                                      stats1[:, j, 1:2], stats1[:, j, 0:1],
                                      ALU.mult, ALU.add)
                    eng.wait_ge(S["s_t3"], j + 1)
                    if j >= 2:
                        eng.wait_ge(S["s_out"], 16 * (j - 1))
                    eng.tensor_add(y2buf[:, j % 2, :], t2buf[:, :],
                                   y1buf[:, :])
                    eng.inc(S["s_y2"], 1)

            @block.scalar
            def _(eng):
                eng = _Rec(eng)
                emit_setup(eng, "scalar")
                setup_barrier(eng)

                acdc = [0]

                def adrain():
                    eng.inc(S["s_acd"], 1)
                    acdc[0] += 1
                    eng.wait_ge(S["s_acd"], acdc[0])

                for it in range(NITER):
                    kf, b = kv(it), bvf(it)
                    k = k2v(it)
                    bw = bwf(it)
                    if kf is not None:
                        p = kf % 2
                        eng.wait_ge(S["s_st0"], kf + 1)
                        eng.wait_ge(S["s_xt"], 16 * kf)
                        if kf >= 4:
                            eng.wait_ge(S["s_vmm"], kf - 3)
                        eng.activation(rstdk[:, p, :], rvk[:, p, :], AF.Ln)
                        eng.activation(rstdk[:, p, :], rstdk[:, p, :],
                                       AF.Exp, scale=-0.5)
                        eng.mul(nmrk[:, p, :], muk[:, p, :], -1.0)
                        eng.mul(nmrk[:, p, :], nmrk[:, p, :],
                                rstdk[:, p, :])
                        adrain()
                        eng.activation(xinb[:, p, :], xbuf[:, kf % 3, :],
                                       AF.Identity, bias=nmrk[:, p, :],
                                       scale=rstdk[:, p, :]
                                       ).then_inc(S["s_ln"], 1)
                        eng.wait_ge(S["s_ln"], kf + 1)
                        eng.dma_start(out=xinT[:, kf % 4, :, :],
                                      in_=xinb[:, p, :]
                                      .rearrange("p (c f) -> p c f", f=128),
                                      transpose=True).then_inc(S["s_xt"], 16)
                    if k is not None:
                        p = k % 2
                        eng.wait_ge(S["s_vmm"], k + 1)
                        if k >= NV:
                            eng.wait_ge(S["s_pv"], k - NV + 1)
                        eng.activation(vwin[:, k % NV, :], ps_y[:, :],
                                       AF.Copy).then_inc(S["s_vev"], 1)
                        eng.activation(sqs[:, :], ps_a[:, 0:64], AF.Square,
                                       accum_out=n2cb[:, p, :])
                        eng.activation(e2q[:, p, :], ps_a[:, 64:65], AF.Exp,
                                       scale=2.0).then_inc(S["s_sqr"], 1)
                        eng.wait_ge(S["s_n2c"], k + 1)
                        eng.activation(rnorm[:, p, :], n2cb[:, p, :], AF.Ln)
                        eng.activation(rnorm[:, p, :], rnorm[:, p, :],
                                       AF.Exp, scale=-0.5)
                        adrain()
                        if k >= 1:
                            eng.wait_ge(S["s_shT"], k)
                        eng.activation(shellU[:, p, :], ps_a[:, 0:64],
                                       AF.Identity, scale=rnorm[:, p, :]
                                       ).then_inc(S["s_shx"], 1)
                    if b is not None:
                        p = b % 2
                        eng.wait_ge(S["s_pst"], b + 1)
                        if b >= 1:
                            eng.wait_ge(S["s_pv"], b)
                        eng.activation(PTt[:, :], scf[:, :], AF.Exp
                                       ).then_inc(S["s_exp"], 1)
                        eng.wait_ge(S["s_zf"], b + 1)
                        eng.wait_ge(S["s_oat"], 16 * b)
                        eng.activation(oa_bf[:, p, :], ps_x[:, :],
                                       AF.Identity, scale=rZb[:, p, :]
                                       ).then_inc(S["s_pvev"], 1)
                        eng.wait_ge(S["s_pvev"], b + 1)
                        if b >= 1:
                            eng.wait_ge(S["s_wmm"], b)
                        eng.dma_start(out=oaT[:, p, :, :],
                                      in_=oa_bf[:, p, :]
                                      .rearrange("p (c f) -> p c f", f=128),
                                      transpose=True).then_inc(S["s_oat"], 16)
                    if bw is not None:
                        p = bw % 2
                        eng.wait_ge(S["s_ast"], bw + 1)
                        eng.activation(rstda[:, p, :], rva[:, p, :], AF.Ln)
                        eng.activation(rstda[:, p, :], rstda[:, p, :],
                                       AF.Exp, scale=-0.5)
                        eng.mul(nmra[:, p, :], mua[:, p, :], -1.0)
                        eng.mul(nmra[:, p, :], nmra[:, p, :],
                                rstda[:, p, :])
                        adrain()
                        eng.activation(t2buf[:, :], tb2[:, :], AF.Identity,
                                       bias=nmra[:, p, :],
                                       scale=rstda[:, p, :]
                                       ).then_inc(S["s_t2e"], 1)
                    if b is not None:
                        if b >= 1:
                            eng.wait_ge(S["s_ast"], b)
                        eng.wait_ge(S["s_wmm"], b + 1)
                        eng.activation(tb2[:, :], ps_y[:, :], AF.Copy
                                       ).then_inc(S["s_wev"], 1)
                    if bw is not None:
                        p = bw % 2
                        eng.wait_ge(S["s_st1"], bw + 1)
                        if bw >= 2:
                            eng.wait_ge(S["s_zhs"], bw - 1)
                            eng.wait_ge(S["s_zsc"], 16 * (bw - 1))
                        eng.activation(r1cb[:, p, :], rv1[:, p, :], AF.Ln)
                        eng.activation(sg1[:, p, :], r1cb[:, p, :], AF.Exp,
                                       scale=0.5)
                        eng.activation(r1cb[:, p, :], r1cb[:, p, :], AF.Exp,
                                       scale=-0.5)
                        eng.mul(nmr1[:, p, :], mu1[:, p, :], -1.0)
                        eng.mul(nmr1[:, p, :], nmr1[:, p, :], r1cb[:, p, :])
                        eng.copy(stats1[:, bw, 0:1], mu1[:, p, :])
                        eng.copy(stats1[:, bw, 1:2], sg1[:, p, :])
                        adrain()
                        eng.activation(zhat_sb[:, p, :], y1buf[:, :],
                                       AF.Identity, bias=nmr1[:, p, :],
                                       scale=r1cb[:, p, :]
                                       ).then_inc(S["s_zh"], 1)

                eng.wait_ge(S["s_r"], 3)
                eng.activation(tb2[0:1, 0:1024], tb2[0:1, 0:1024], AF.Ln)
                eng.activation(tb2[0:1, 0:1024], tb2[0:1, 0:1024], AF.Exp,
                               scale=-0.5).then_inc(S["s_r"], 1)
                eng.wait_ge(S["s_r"], 7)
                eng.activation(r8a[:, :], r8a[:, :], AF.Exp,
                               bias=r1b[0:1, 0:1]).then_inc(S["s_r"], 1)

                eng.wait_ge(S["s_r"], 14)
                eng.wait_ge(S["s_rb"], 16 * 6)
                for j in range(NQB):
                    eng.wait_ge(S["s_h"], j + 1)
                    eng.activation(hTt[:, j % 2, :], ps_a[0:64, 0:128],
                                   AF.Relu, bias=bdcol[:, :]
                                   ).then_inc(S["s_hev"], 1)
                    eng.wait_ge(S["s_ymm"], j + 1)
                    eng.activation(tb2[:, :], ps_x[:, :], AF.Identity,
                                   scale=gcol[:, :]).then_inc(S["s_tg"], 1)
                    eng.wait_ge(S["s_yst"], j + 1)
                    pj = j % 2
                    eng.activation(rYc[:, pj, :], rvy[:, pj, :], AF.Ln)
                    eng.activation(rYc[:, pj, :], rYc[:, pj, :], AF.Exp,
                                   scale=-0.5)
                    eng.mul(nmry[:, pj, :], muy[:, pj, :], -1.0)
                    eng.mul(nmry[:, pj, :], nmry[:, pj, :], rYc[:, pj, :])
                    adrain()
                    eng.activation(t2buf[:, :], tb2[:, :], AF.Identity,
                                   bias=nmry[:, pj, :],
                                   scale=rYc[:, pj, :]
                                   ).then_inc(S["s_t3"], 1)

            @block.gpsimd
            def _(eng):
                eng = _Rec(eng)
                setup_barrier(eng)
                for b in range(NQB):
                    eng.wait_ge(S["s_zh"], b + 1)
                    eng.wait_ge(S["s_zsc"], 16 * b)
                    eng.dma_start(out=zhat_dram[128 * b:128 * (b + 1), :],
                                  in_=zhat_sb[:, b % 2, :]
                                  ).then_inc(S["s_zsc"], 16)
                eng.wait_ge(S["s_r"], 5)
                eng.dma_start(out=rb_dram[0:1, 0:1024],
                              in_=stage[0:1, 0:1024]).then_inc(S["s_rb"], 16)
                eng.wait_ge(S["s_rb"], 16 * 1)
                eng.dma_start(out=zbnT[:, :],
                              in_=rb_dram[0:1, :]
                              .rearrange("q (c p) -> (q p) c", p=128),
                              allow_slow_non_contiguous=True
                              ).then_inc(S["s_rb"], 16)
                eng.wait_ge(S["s_r"], 9)
                eng.wait_ge(S["s_rb"], 16 * 2)
                eng.dma_start(out=rb_dram[0:1, 0:8],
                              in_=stage2[0:1, 16:24]).then_inc(S["s_rb"], 16)
                eng.wait_ge(S["s_rb"], 16 * 3)
                eng.dma_start(out=ohc[:, :],
                              in_=rb_dram[0:1, 0:8]
                              .rearrange("q (c o) -> (q c) o", o=1)
                              ).then_inc(S["s_rb"], 16)
                eng.wait_ge(S["s_r"], 13)
                eng.wait_ge(S["s_rb"], 16 * 4)
                eng.dma_start(out=rb_dram[0:1, 0:64],
                              in_=stage2[0:1, 32:96]).then_inc(S["s_rb"], 16)
                eng.wait_ge(S["s_rb"], 16 * 5)
                eng.dma_start(out=bdcol[:, :],
                              in_=rb_dram[0:1, 0:64]
                              .rearrange("q (c o) -> (q c) o", o=1)
                              ).then_inc(S["s_rb"], 16)
                for j in range(NQB):
                    eng.wait_ge(S["s_y2"], j + 1)
                    eng.wait_ge(S["s_out"], 16 * j)
                    eng.dma_start(out=y_out[128 * j:128 * (j + 1), :],
                                  in_=y2buf[:, j % 2, :]
                                  ).then_inc(S["s_out"], 16)
                eng.wait_ge(S["s_out"], 16 * NQB)
                if cfg.get("DEBUG"):
                    nrb = [0]
                    for nm, ap in dbg_list:
                        eng.wait_ge(S["s_rb"], 16 * (5 + nrb[0]))
                        eng.dma_start(out=dbg_outs[nm], in_=ap
                                      ).then_inc(S["s_rb"], 16)
                        nrb[0] += 1

    return nc



_NC_CACHE = {}


def _get_nc(seq_len):
    if seq_len not in _NC_CACHE:
        _NC_CACHE[seq_len] = build_nc(make_cfg(seq_len))
    return _NC_CACHE[seq_len]


def kernel(**inputs):
    x = np.asarray(inputs["L0_attn_x"])
    B, N, D = x.shape
    cfg = make_cfg(N)
    consts = host_constants(cfg)
    ws = stage_weights({k: np.asarray(v) for k, v in inputs.items()})
    nc = _get_nc(N)

    base = {}
    for nm, (shape, dty) in IN_SPECS.items():
        src = ws.get(nm)
        if src is None:
            src = consts[nm]
        arr = np.asarray(src)
        if dty == F32:
            arr = np.ascontiguousarray(arr, dtype=np.float32)
        else:
            arr = np.ascontiguousarray(arr, dtype=ml_dtypes.bfloat16)
        assert list(arr.shape) == shape, (nm, arr.shape, shape)
        base[nm] = arr

    pad = np.zeros((512, D), np.float32)
    in_maps = []
    for c in range(8):
        seq = c // 2
        xk = np.ascontiguousarray(
            np.concatenate([pad, np.asarray(x[seq], np.float32)], axis=0))
        m = dict(base)
        m["x_k"] = xk
        in_maps.append(m)

    res = run_bass_kernel_spmd(nc, in_maps, core_ids=list(range(8)))
    out = np.empty((B, N, D), np.float32)
    for seq in range(B):
        out[seq] = res.results[2 * seq]["y_out"]
    return out
